# version 07
# baseline (speedup 1.0000x reference)
import math
from contextlib import ExitStack

import numpy as np
import ml_dtypes

import concourse.bass as bass
import concourse.mybir as mybir
from concourse.bass_utils import run_bass_kernel_spmd

F32 = mybir.dt.float32
BF16 = mybir.dt.bfloat16
I16 = mybir.dt.int16
I32 = mybir.dt.int32
AF = mybir.ActivationFunctionType
ALU = mybir.AluOpType
AXX = mybir.AxisListType.X

B, S, D = 2, 8192, 1024
H = 16
NC, CT = 16, 512
HE = 4
QSC = 4096.0
INV8 = 0.125
SHIFT = 8.0
TWO_PI = 2.0 * math.pi
INV2PI = float(np.float32(1.0 / (2.0 * math.pi)))
_b = np.float32(INV2PI).view(np.int32) & np.int32(-4096)
C_HI = float(np.int32(_b).view(np.float32))
C_LO = float(np.float32(1.0 / (2.0 * math.pi) - C_HI))

CHUNKS = [sorted([bc, 7 - bc, 8 + bc, 15 - bc]) for bc in range(4)]
DEBUG = False


class Seq:

    def __init__(self, sem):
        self.sem = sem
        self.count = 0

    def run(self, eng, fn, inc=1):
        if self.count:
            eng.wait_ge(self.sem, self.count)
        fn().then_inc(self.sem, inc)
        self.count += inc


def build_nc():
    nc = bass.Bass("TRN2", target_bir_lowering=False)

    ap = lambda t: t.ap()
    xT = ap(nc.dram_tensor("xT", [D, S], F32, kind="ExternalInput"))
    wqkvT = ap(nc.dram_tensor("wqkvT", [D, 3 * D], F32, kind="ExternalInput"))
    wo_r = ap(nc.dram_tensor("wo_r", [8, D, 128], F32, kind="ExternalInput"))
    gates3 = ap(nc.dram_tensor("gates3", [16, 3], F32, kind="ExternalInput"))
    wg_d = ap(nc.dram_tensor("wg_d", [D, 4096], F32, kind="ExternalInput"))
    wu_d = ap(nc.dram_tensor("wu_d", [D, 4096], F32, kind="ExternalInput"))
    wd_d = ap(nc.dram_tensor("wd_d", [4096, D], F32, kind="ExternalInput"))
    bg_d = ap(nc.dram_tensor("bg_d", [128, 32], F32, kind="ExternalInput"))
    bu_d = ap(nc.dram_tensor("bu_d", [128, 32], F32, kind="ExternalInput"))
    bd_d = ap(nc.dram_tensor("bd_d", [128, 8], F32, kind="ExternalInput"))
    wq_r = ap(nc.dram_tensor("wq_r", [4, D, 128], F32, kind="ExternalInput"))
    kbd_d = ap(nc.dram_tensor("kbd_d", [512, 512], F32, kind="ExternalInput"))
    deT_d = ap(nc.dram_tensor("deT_d", [D, 4096], F32, kind="ExternalInput"))
    ue_d = ap(nc.dram_tensor("ue_d", [4096, D], F32, kind="ExternalInput"))
    th0_d = ap(nc.dram_tensor("th0_d", [128, S], F32, kind="ExternalInput"))
    qpos_d = ap(nc.dram_tensor("qpos_d", [16, 4 * CT], F32, kind="ExternalInput"))
    kpos_d = ap(nc.dram_tensor("kpos_d", [128, 64], F32, kind="ExternalInput"))
    rsel_d = ap(nc.dram_tensor("rsel_d", [16, 4 * 128], F32, kind="ExternalInput"))
    cmask_d = ap(nc.dram_tensor("cmask_d", [128, 4 * CT], BF16, kind="ExternalInput"))
    ident_d = ap(nc.dram_tensor("ident_d", [128, 128], BF16, kind="ExternalInput"))

    out_d = ap(nc.dram_tensor("out", [4, D, CT], F32, kind="ExternalOutput"))

    kind = "ExternalOutput" if DEBUG else "Internal"
    k_dram = ap(nc.dram_tensor("k_dram", [D, S], BF16, kind=kind))
    v_dram = ap(nc.dram_tensor("v_dram", [4, 64, 128, 4 * 65], BF16, kind=kind))
    q_dram = ap(nc.dram_tensor("q_dram", [D, 4 * CT], BF16, kind=kind))
    g_dram = ap(nc.dram_tensor("g_dram", [3, 16, 4 * CT], BF16, kind=kind))
    srow_dram = ap(nc.dram_tensor("srow_dram", [2, 16], F32))
    gate_dram = ap(nc.dram_tensor("gate_dram", [4096, CT], BF16, kind=kind))
    a_dram = ap(nc.dram_tensor("a_dram", [4096, CT], BF16, kind=kind))
    m_dram = ap(nc.dram_tensor("m_dram", [4096, CT], BF16, kind=kind))
    dbg_x2 = ap(nc.dram_tensor("dbg_x2", [4, D, CT], F32, kind=kind)) \
        if DEBUG else None
    dbg_kc = ap(nc.dram_tensor("dbg_kc", [128, 1024], F32, kind=kind)) \
        if DEBUG else None
    dbg_o = ap(nc.dram_tensor("dbg_o", [4, 128, 8 * CT], BF16, kind=kind)) \
        if DEBUG else None
    dbg_p = ap(nc.dram_tensor("dbg_p", [64, 128, CT], BF16, kind=kind)) \
        if DEBUG else None
    dbg_s = ap(nc.dram_tensor("dbg_s", [64, 128, CT], F32, kind=kind)) \
        if DEBUG else None

    with ExitStack() as st0:
        def mk(stack, sfx=""):
            def sb(name, shape, dtype):
                return stack.enter_context(
                    nc.sbuf_tensor(name + sfx, shape, dtype))
            return sb

        sb0 = mk(st0)
        ones1f = sb0("ones1f", [1, 128], F32)
        onesb = sb0("onesb", [128, 1], BF16)
        eps_t = sb0("eps_t", [1, 1], F32)
        ident = sb0("ident", [128, 128], BF16)
        gsig = sb0("gsig", [16, 3], F32)
        xi_r = sb0("xi_r", [16, 1], F32)
        pig = sb0("pig", [16, 1], F32)
        egf = sb0("egf", [16, 1], F32)
        chh = sb0("chh", [16, 1], F32)
        cm8 = sb0("cm8", [16, 1], F32)
        tmp16 = sb0("tmp16", [16, 4], F32)
        pigcol = sb0("pigcol", [128, 4], F32)
        crep = sb0("crep", [128, 16], F32)
        kc = sb0("kc", [128, 64 * 16], F32)
        cmask = sb0("cmask", [128, 4 * CT], BF16)
        rows = sb0("rows", [1, 6 * CT], F32)
        rb_sb = sb0("rb_sb", [128, CT], F32)
        mrb_sb = sb0("mrb_sb", [128, CT], F32)
        xb1 = sb0("xb1", [128, CT], BF16)
        sq1 = sb0("sq1", [128, CT], BF16)
        xc = sb0("xc", [128, 8 * CT], F32)

        ps_a = st0.enter_context(nc.psum_tensor("ps_a", [128, 512], F32))
        ps_b = st0.enter_context(nc.psum_tensor("ps_b", [128, 512], F32))
        ps_s0 = st0.enter_context(nc.psum_tensor("ps_s0", [128, 512], F32))
        ps_s1 = st0.enter_context(nc.psum_tensor("ps_s1", [128, 512], F32))
        ps_acc = st0.enter_context(nc.psum_tensor("ps_acc", [65, 512], F32))
        ps_st0 = st0.enter_context(nc.psum_tensor("ps_st0", [1, 512], F32))
        ps_st1 = st0.enter_context(nc.psum_tensor("ps_st1", [1, 512], F32))
        ps_tp = st0.enter_context(nc.psum_tensor("ps_tp", [128, 1024], BF16))

        gsem = st0.enter_context(nc.semaphore("gs"))
        blk = st0.enter_context(nc.Block())

        V, SC, T, G, SY = nc.vector, nc.scalar, nc.tensor, nc.gpsimd, nc.sync

        M_ROW = rows[:, 0:CT]
        V_ROW = rows[:, CT:2 * CT]
        R_ROW = rows[:, 2 * CT:3 * CT]
        MR_ROW = rows[:, 3 * CT:4 * CT]
        Z_ROW = rows[:, 4 * CT:5 * CT]

        engs = [V, SC, T, G, SY]
        idxs = [e.compute_val(e.partition_id() % 4) for e in engs]

        for bc in nc.Switch(engines=engs, index=idxs, n=4):
            s = Seq(gsem)
            ch_list = CHUNKS[bc]

            def LD(dst, src, eng=SY):
                s.run(eng, lambda: eng.dma_start(dst, src), 16)

            def MM(out, lhsT, rhs, start, stop):
                s.run(T, lambda: T.matmul(out, lhsT, rhs, start=start,
                                          stop=stop))

            def VOP(fn):
                s.run(V, fn)

            def ACT(out, in_, func, bias=0.0, scale=1.0):
                s.run(SC, lambda: SC.activation(out, in_, func, bias=bias,
                                                scale=scale))

            VOP(lambda: V.memset(ones1f[:], 1.0))
            VOP(lambda: V.memset(onesb[:], 1.0))
            VOP(lambda: V.memset(eps_t[:], 1e-5))
            LD(ident[:], ident_d)
            LD(gsig[:], gates3)
            LD(rb_sb[:, 0:64], kpos_d)
            LD(cmask[:], cmask_d)
            ACT(xi_r[:], gsig[:, 0:1], AF.Exp)
            VOP(lambda: V.reciprocal(xi_r[:], xi_r[:]))
            for col, dst in ((1, pig), (2, egf)):
                ACT(dst[:], gsig[:, col:col + 1], AF.Exp, scale=-1.0)
                VOP(lambda d=dst: V.tensor_scalar_add(d[:], d[:], 1.0))
                VOP(lambda d=dst: V.reciprocal(d[:], d[:]))
            VOP(lambda: V.tensor_mul(chh[:], egf[:], xi_r[:]))
            VOP(lambda: V.tensor_scalar_mul(cm8[:], chh[:], -8.0))
            VOP(lambda: V.tensor_copy(tmp16[:, 0:1], cm8[:]))
            VOP(lambda: V.tensor_copy(tmp16[:, 1:2], chh[:]))
            s.run(SY, lambda: SY.dma_start(
                srow_dram.rearrange("a p -> p a"), tmp16[:, 0:2],
                allow_slow_non_contiguous=True), 16)
            LD(crep[0:1, 0:16], srow_dram[1:2, :])
            MM(ps_a[:, 0:16], ones1f[:], crep[0:1, 0:16], True, True)
            VOP(lambda: V.tensor_copy(crep[:], ps_a[:, 0:16]))
            VOP(lambda: V.tensor_tensor(
                kc[:].rearrange("p (k h) -> p k h", h=16),
                rb_sb[:, 0:64].rearrange("p (k o) -> p k o", o=1)
                .to_broadcast([128, 64, 16]),
                crep[:].rearrange("p (o h) -> p o h", o=1)
                .to_broadcast([128, 64, 16]),
                ALU.mult))
            VOP(lambda: V.tensor_scalar_add(kc[:], kc[:], -SHIFT))
            if DEBUG:
                LD(dbg_kc, kc[:])
            VOP(lambda: V.tensor_copy(tmp16[:, 2:3], pig[:]))
            with ExitStack() as st1:
                sb1 = mk(st1, f"_{bc}")
                rsel = sb1("rsel", [16, 4 * 128], F32)
                qpos = sb1("qpos", [16, 4 * CT], F32)
                g_pm = sb1("g_pm", [16, 3 * 4 * CT], BF16)
                gt1 = sb1("gt1", [16, 4 * CT], F32)
                gt2 = sb1("gt2", [16, 4 * CT], F32)
                LD(rsel[:], rsel_d)
                LD(qpos[:], qpos_d)
                for g in range(4):
                    MM(ps_a[:, g:g + 1], rsel[:, g * 128:(g + 1) * 128],
                       tmp16[:, 2:3], True, True)
                VOP(lambda: V.tensor_copy(pigcol[:], ps_a[:, 0:4]))
                VOP(lambda: V.tensor_scalar(gt1[:], qpos[:], cm8[:, 0:1],
                                            None, op0=ALU.mult))
                VOP(lambda: V.tensor_copy(g_pm[:, 0:2048], gt1[:]))
                VOP(lambda: V.tensor_sub(gt2[:], gt1[:], g_pm[:, 0:2048]))
                VOP(lambda: V.tensor_copy(g_pm[:, 2048:4096], gt2[:]))
                VOP(lambda: V.tensor_sub(gt1[:], gt2[:],
                                         g_pm[:, 2048:4096]))
                VOP(lambda: V.tensor_copy(g_pm[:, 4096:6144], gt1[:]))
                LD(g_dram.rearrange("r h t -> h r t"),
                   g_pm[:].rearrange("h (r t) -> h r t", r=3))

            def layernorm(src, out_bf):
                for dc in range(8):
                    sl = slice(dc * CT, (dc + 1) * CT)
                    VOP(lambda sl=sl: V.tensor_copy(xb1[:], src[:, sl]))
                    VOP(lambda: V.tensor_mul(sq1[:], xb1[:], xb1[:]))
                    MM(ps_st0[:], onesb[:], xb1[:], dc == 0, dc == 7)
                    MM(ps_st1[:], onesb[:], sq1[:], dc == 0, dc == 7)
                VOP(lambda: V.tensor_scalar_mul(M_ROW, ps_st0[:], 1.0 / D))
                VOP(lambda: V.tensor_scalar_mul(V_ROW, ps_st1[:], 1.0 / D))
                VOP(lambda: V.tensor_mul(Z_ROW, M_ROW, M_ROW))
                VOP(lambda: V.tensor_sub(V_ROW, V_ROW, Z_ROW))
                ACT(V_ROW, V_ROW, AF.Ln, bias=eps_t[0:1, 0:1])
                ACT(R_ROW, V_ROW, AF.Exp, scale=-0.5)
                VOP(lambda: V.tensor_mul(MR_ROW, M_ROW, R_ROW))
                VOP(lambda: V.tensor_scalar_mul(MR_ROW, MR_ROW, -1.0))
                MM(ps_a[:, 0:CT], ones1f[:], R_ROW, True, True)
                MM(ps_b[:, 0:CT], ones1f[:], MR_ROW, True, True)
                VOP(lambda: V.tensor_copy(rb_sb[:], ps_a[:, 0:CT]))
                VOP(lambda: V.tensor_copy(mrb_sb[:], ps_b[:, 0:CT]))
                for dc in range(8):
                    sl = slice(dc * CT, (dc + 1) * CT)
                    VOP(lambda sl=sl: V.tensor_tensor(
                        out_bf[:, sl], src[:, sl], rb_sb[:], ALU.mult))
                    VOP(lambda sl=sl: V.tensor_tensor(
                        out_bf[:, sl], out_bf[:, sl], mrb_sb[:], ALU.add))

            with ExitStack() as st2:
                sb2 = mk(st2, f"_{bc}")
                wqkv_sb = sb2("wqkv_sb", [128, 8 * 3072], BF16)
                lnb = sb2("lnb", [128, 8 * CT], BF16)
                thf = sb2("thf", [128, 4 * CT], F32)
                rnd32 = sb2("rnd32", [128, 4 * CT], I32)
                frac = sb2("frac", [128, 4 * CT], F32)
                sc2 = sb2("sc2", [128, 4 * CT], F32)
                sc3 = sb2("sc3", [128, 4 * CT], F32)
                csnb = sb2("csnb", [128, 4 * CT], BF16)
                ccsb = sb2("ccsb", [128, 4 * CT], BF16)
                kqA = sb2("kqA", [128, CT], BF16)
                kqB = sb2("kqB", [128, CT], BF16)
                rotA = sb2("rotA", [128, CT], BF16)
                rotB = sb2("rotB", [128, CT], BF16)
                rtmp = sb2("rtmp", [128, CT], BF16)
                v_loc = sb2("v_loc", [128, 4 * 16 * 65], BF16)

                LD(wqkv_sb[:].rearrange("p (c n) -> p c n", c=8),
                   wqkvT.rearrange("(c p) n -> p c n", p=128), G)

                for tc in range(NC):
                    mine = tc in ch_list
                    mci = ch_list.index(tc) if mine else -1
                    LD(xc[:].rearrange("p (c t) -> p c t", c=8),
                       xT.rearrange("(c p) t -> p c t", p=128)
                       [:, :, tc * CT:(tc + 1) * CT])
                    layernorm(xc[:], lnb)

                    LD(sc3[:, 0:CT], th0_d[:, tc * CT:(tc + 1) * CT])
                    VOP(lambda: V.tensor_tensor(
                        thf[:].rearrange("p (g t) -> p g t", g=4),
                        sc3[:, 0:CT].rearrange("p (o t) -> p o t", o=1)
                        .to_broadcast([128, 4, CT]),
                        pigcol[:].rearrange("p (g o) -> p g o", o=1)
                        .to_broadcast([128, 4, CT]),
                        ALU.mult))
                    VOP(lambda: V.tensor_single_scalar(
                        sc2[:].bitcast(I32), thf[:].bitcast(I32), -4096,
                        ALU.bitwise_and))
                    VOP(lambda: V.tensor_sub(sc3[:], thf[:], sc2[:]))
                    VOP(lambda: V.tensor_scalar_mul(thf[:], sc2[:], C_HI))
                    VOP(lambda: V.tensor_copy(rnd32[:], thf[:]))
                    VOP(lambda: V.tensor_copy(frac[:], rnd32[:]))
                    VOP(lambda: V.tensor_sub(thf[:], thf[:], frac[:]))
                    VOP(lambda: V.tensor_scalar_mul(sc2[:], sc2[:], C_LO))
                    VOP(lambda: V.tensor_scalar_mul(sc3[:], sc3[:], INV2PI))
                    VOP(lambda: V.tensor_add(thf[:], thf[:], sc2[:]))
                    VOP(lambda: V.tensor_add(thf[:], thf[:], sc3[:]))
                    VOP(lambda: V.tensor_copy(rnd32[:], thf[:]))
                    VOP(lambda: V.tensor_copy(frac[:], rnd32[:]))
                    VOP(lambda: V.tensor_sub(frac[:], thf[:], frac[:]))
                    ACT(csnb[:], frac[:], AF.Sin, scale=TWO_PI)
                    VOP(lambda: V.tensor_scalar_add(frac[:], frac[:], 0.25))
                    VOP(lambda: V.tensor_scalar(thf[:], frac[:], 0.5, None,
                                                op0=ALU.is_gt))
                    VOP(lambda: V.tensor_sub(frac[:], frac[:], thf[:]))
                    ACT(ccsb[:], frac[:], AF.Sin, scale=TWO_PI)

                    def proj_rot(col_base, dst_dram, dst_col):
                        for g in range(4):
                            for half, dst_ps in ((0, ps_a), (1, ps_b)):
                                cb = col_base + g * 256 + half * 128
                                for dc in range(8):
                                    MM(dst_ps[:, 0:CT],
                                       wqkv_sb[:, dc * 3072 + cb:
                                               dc * 3072 + cb + 128],
                                       lnb[:, dc * CT:(dc + 1) * CT],
                                       dc == 0, dc == 7)
                            VOP(lambda: V.tensor_copy(kqA[:], ps_a[:, 0:CT]))
                            VOP(lambda: V.tensor_copy(kqB[:], ps_b[:, 0:CT]))
                            gs = slice(g * CT, (g + 1) * CT)
                            VOP(lambda gs=gs: V.tensor_tensor(
                                rotA[:], kqA[:], ccsb[:, gs], ALU.mult))
                            VOP(lambda gs=gs: V.tensor_tensor(
                                rtmp[:], kqB[:], csnb[:, gs], ALU.mult))
                            VOP(lambda: V.tensor_sub(rotA[:], rotA[:],
                                                     rtmp[:]))
                            VOP(lambda gs=gs: V.tensor_tensor(
                                rotB[:], kqA[:], csnb[:, gs], ALU.mult))
                            VOP(lambda gs=gs: V.tensor_tensor(
                                rtmp[:], kqB[:], ccsb[:, gs], ALU.mult))
                            VOP(lambda: V.tensor_add(rotB[:], rotB[:],
                                                     rtmp[:]))
                            for half, rsrc in ((0, rotA), (1, rotB)):
                                LD(dst_dram.rearrange(
                                    "(hh r) t -> hh r t", r=64)
                                   [g * 4:(g + 1) * 4,
                                    half * 32:half * 32 + 32,
                                    dst_col:dst_col + CT],
                                   rsrc[:])

                    proj_rot(1024, k_dram, tc * CT)
                    if mine:
                        proj_rot(0, q_dram, mci * CT)

                    for kcq in range(4):
                        for half, dst_ps in ((0, ps_a), (1, ps_b)):
                            for dc in range(8):
                                MM(dst_ps[:, 0:512],
                                   lnb[:, dc * CT + kcq * 128:
                                       dc * CT + kcq * 128 + 128],
                                   wqkv_sb[:, dc * 3072 + 2048 + half * 512:
                                           dc * 3072 + 2048 + half * 512
                                           + 512],
                                   dc == 0, dc == 7)
                        for half, src_ps in ((0, ps_a), (1, ps_b)):
                            VOP(lambda kcq=kcq, half=half, src_ps=src_ps:
                                V.tensor_copy(
                                    bass.AP(v_loc, kcq * 1040 + half * 520,
                                            [[4160, 128], [65, 8], [1, 64]]),
                                    src_ps[:, 0:512].rearrange(
                                        "p (h d) -> p h d", h=8)))
                    VOP(lambda: V.memset(
                        bass.AP(v_loc, 64, [[4160, 128], [65, 64], [1, 1]]),
                        1.0))
                    for g in range(4):
                        LD(v_dram[g].rearrange("k p c -> p k c")
                           [:, tc * 4:(tc + 1) * 4, :],
                           v_loc[:].rearrange("p (k g c) -> p k g c",
                                              k=4, g=4)[:, :, g, :])

            with ExitStack() as st3:
                sb3 = mk(st3, f"_{bc}")
                wob = sb3("wob", [128, 8 * 128], BF16)
                wqb = sb3("wqb", [128, 8 * 128], BF16)
                kbd_sb = sb3("kbd_sb", [128, 4 * 512], BF16)
                bg_sb = sb3("bg_sb", [128, 32], F32)
                bu_sb = sb3("bu_sb", [128, 32], F32)
                bd_sb = sb3("bd_sb", [128, 8], F32)
                q_sub = sb3("q_sub", [67, CT], BF16)
                k_sub = sb3("k_sub", [67, 1024], BF16)
                v_sub = sb3("v_sub", [128, 8 * 260], BF16)
                p_t = sb3("p_t", [128, CT], BF16)
                rb64 = sb3("rb64", [64, CT], F32)
                o_fm = sb3("o_fm", [128, 8 * CT], BF16)
                x2 = sb3("x2", [128, 8 * CT], F32)
                h_mc = sb3("h_mc", [128, 8 * CT], BF16)
                wgu = sb3("wgu", [128, 2 * 1024], BF16)
                wdb = sb3("wdb", [128, 1024], BF16)
                gch = sb3("gch", [128, CT], BF16)
                uch = sb3("uch", [128, CT], BF16)
                mch = sb3("mch", [128, CT], BF16)
                qe_fm = sb3("qe_fm", [128, 4 * CT], BF16)
                sim_f = sb3("sim_f", [128, 512], F32)
                sxq = sb3("sxq", [128, 512], I16)
                sxf = sb3("sxf", [128, 512], F32)
                v8 = sb3("v8", [128, 8 * HE], F32)
                vy8 = sb3("vy8", [128, 8 * HE], F32)
                combo = sb3("combo", [128, 64 * HE], F32)
                tau8 = sb3("tau8", [128, HE], F32)
                taub = sb3("taub", [128, HE], F32)
                e8 = sb3("e8", [128, 8 * HE], F32)
                zsum = sb3("zsum", [128, HE], F32)
                kt = sb3("kt", [128, HE], F32)
                exab = sb3("exab", [128, 64 * HE], BF16)
                exbb = sb3("exbb", [128, 64 * HE], BF16)
                d16 = sb3("d16", [128, 4096], I16)
                mkb = sb3("mkb", [128, 4096], BF16)
                prodb = sb3("prodb", [128, 4096], BF16)
                planeb = sb3("planeb", [128, 4096], BF16)
                gacc = sb3("gacc", [128, 4096], BF16)
                tevac = sb3("tevac", [128, 1024], BF16)
                ueb = sb3("ueb", [128, 1024], BF16)
                deb = sb3("deb", [128, 8 * 128], BF16)
                wsil = sb3("wsil", [128, CT], BF16)
                gld = sb3("gld", [128, CT], BF16)
                mld = sb3("mld", [128, CT], BF16)

                VOP(lambda: V.memset(k_sub[64:67, :], 1.0))

                LD(kbd_sb[:].rearrange("p (c n) -> p c n", c=4),
                   kbd_d.rearrange("(c p) n -> p c n", p=128), G)
                LD(bg_sb[:], bg_d)
                LD(bu_sb[:], bu_d)
                LD(bd_sb[:], bd_d)

                for mci in range(4):
                    mc = ch_list[mci]
                    nkc = (mc + 1) * 4
                    for h in range(H):
                        grp = h // 4
                        hl = h % 4
                        LD(q_sub[0:64, :],
                           q_dram[h * 64:(h + 1) * 64,
                                  mci * CT:(mci + 1) * CT])
                        LD(q_sub[64:67, :],
                           g_dram[:, h, mci * CT:(mci + 1) * CT])
                        first = True
                        for ks in range((nkc + 7) // 8):
                            kb0 = ks * 8
                            kb1 = min(nkc, kb0 + 8)
                            LD(k_sub[0:64, 0:(kb1 - kb0) * 128],
                               k_dram[h * 64:(h + 1) * 64,
                                      kb0 * 128:kb1 * 128])
                            LD(v_sub[:].rearrange("p (k c) -> p k c", c=260)
                               [:, 0:kb1 - kb0, :],
                               v_dram[grp, kb0:kb1, :, :].rearrange(
                                   "k p c -> p k c"))
                            for kb in range(kb0, kb1):
                                MM(ps_s0[:, 0:CT],
                                   k_sub[0:67,
                                         (kb - kb0) * 128:(kb - kb0 + 1) * 128],
                                   q_sub[0:67, :], True, True)
                                ACT(p_t[:], ps_s0[:, 0:CT], AF.Exp,
                                    bias=kc[:, kb * 16 + h:kb * 16 + h + 1],
                                    scale=INV8)
                                dv = kb - mc * 4
                                if 0 <= dv < 4:
                                    VOP(lambda dv=dv: V.tensor_tensor(
                                        p_t[:], p_t[:],
                                        cmask[:, dv * CT:(dv + 1) * CT],
                                        ALU.mult))
                                if DEBUG and mc == ch_list[3] and h == 1:
                                    LD(dbg_p[kb], p_t[:])
                                    s.run(V, lambda: V.tensor_copy(
                                        rb_sb[:], ps_s0[:, 0:CT]))
                                    LD(dbg_s[kb], rb_sb[:])
                                MM(ps_acc[:, 0:CT],
                                   v_sub[:, (kb - kb0) * 260 + hl * 65:
                                         (kb - kb0) * 260 + hl * 65 + 65],
                                   p_t[:], first, kb == nkc - 1)
                                first = False
                        VOP(lambda: V.reciprocal(Z_ROW, ps_acc[64:65, 0:CT]))
                        MM(ps_b[0:64, 0:CT], ones1f[:, 0:64], Z_ROW,
                           True, True)
                        VOP(lambda: V.tensor_copy(rb64[:], ps_b[0:64, 0:CT]))
                        oc, ob = h // 2, (h % 2) * 64
                        VOP(lambda oc=oc, ob=ob: V.tensor_tensor(
                            o_fm[ob:ob + 64, oc * CT:(oc + 1) * CT],
                            ps_acc[0:64, 0:CT], rb64[:], ALU.mult))

                    if DEBUG:
                        LD(dbg_o[mci], o_fm[:])
                    LD(xc[:].rearrange("p (c t) -> p c t", c=8),
                       xT.rearrange("(c p) t -> p c t", p=128)
                       [:, :, mc * CT:(mc + 1) * CT])
                    for dc in range(8):
                        LD(wob[:].rearrange("p (c n) -> p c n", c=8),
                           wo_r[dc].rearrange("(c p) n -> p c n", p=128), G)
                        for ic in range(8):
                            MM(ps_a[:, 0:CT],
                               wob[:, ic * 128:(ic + 1) * 128],
                               o_fm[:, ic * CT:(ic + 1) * CT],
                               ic == 0, ic == 7)
                        VOP(lambda dc=dc: V.tensor_tensor(
                            x2[:, dc * CT:(dc + 1) * CT], ps_a[:, 0:CT],
                            xc[:, dc * CT:(dc + 1) * CT], ALU.add))
                    if DEBUG:
                        LD(dbg_x2.rearrange("m (c p) t -> m p c t", p=128)
                           [mci], x2[:].rearrange("p (c t) -> p c t", c=8))
                    layernorm(x2[:], h_mc)

                    for dcd in range(32):
                        LD(wgu[:, 0:1024].rearrange("p (c n) -> p c n", c=8),
                           wg_d.rearrange("(c p) n -> p c n", p=128)
                           [:, :, dcd * 128:(dcd + 1) * 128], G)
                        LD(wgu[:, 1024:2048].rearrange("p (c n) -> p c n",
                                                       c=8),
                           wu_d.rearrange("(c p) n -> p c n", p=128)
                           [:, :, dcd * 128:(dcd + 1) * 128], G)
                        for base, dst_ps in ((0, ps_a), (1024, ps_b)):
                            for dc in range(8):
                                MM(dst_ps[:, 0:CT],
                                   wgu[:, base + dc * 128:base + dc * 128
                                       + 128],
                                   h_mc[:, dc * CT:(dc + 1) * CT],
                                   dc == 0, dc == 7)
                        ACT(gch[:], ps_a[:, 0:CT], AF.Silu,
                            bias=bg_sb[:, dcd:dcd + 1])
                        ACT(uch[:], ps_b[:, 0:CT], AF.Identity,
                            bias=bu_sb[:, dcd:dcd + 1])
                        VOP(lambda: V.tensor_mul(mch[:], gch[:], uch[:]))
                        LD(m_dram[dcd * 128:(dcd + 1) * 128, :], mch[:])

                    for qc in range(4):
                        LD(wqb[:].rearrange("p (c n) -> p c n", c=8),
                           wq_r[qc].rearrange("(c p) n -> p c n", p=128), G)
                        for dc in range(8):
                            MM(ps_a[:, 0:CT],
                               wqb[:, dc * 128:(dc + 1) * 128],
                               h_mc[:, dc * CT:(dc + 1) * CT],
                               dc == 0, dc == 7)
                        VOP(lambda qc=qc: V.tensor_copy(
                            qe_fm[:, qc * CT:(qc + 1) * CT], ps_a[:, 0:CT]))
                    for tt in range(4):
                        for qc in range(4):
                            MM(ps_b[:, 0:512],
                               qe_fm[:, qc * CT + tt * 128:
                                     qc * CT + tt * 128 + 128],
                               kbd_sb[:, qc * 512:(qc + 1) * 512],
                               qc == 0, qc == 3)
                        VOP(lambda: V.tensor_copy(sim_f[:], ps_b[:, 0:512]))
                        VOP(lambda: V.tensor_scalar_mul(sxf[:], sim_f[:],
                                                        QSC))
                        VOP(lambda: V.tensor_copy(sxq[:], sxf[:]))
                        VOP(lambda: V.tensor_copy(sxf[:], sxq[:]))
                        for he in range(HE):
                            VOP(lambda he=he: V.max(
                                out=v8[:, he * 8:he * 8 + 8],
                                in_=sxf[:, he * 64:he * 64 + 64]))
                            VOP(lambda he=he: V.max(
                                out=vy8[:, he * 8:he * 8 + 8],
                                in_=sxf[:, 256 + he * 64:256 + he * 64 + 64]))
                            VOP(lambda he=he: V.tensor_tensor(
                                combo[:, he * 64:(he + 1) * 64].rearrange(
                                    "p (a b) -> p a b", b=8),
                                v8[:, he * 8:he * 8 + 8].rearrange(
                                    "p (a o) -> p a o", o=1)
                                .to_broadcast([128, 8, 8]),
                                vy8[:, he * 8:he * 8 + 8].rearrange(
                                    "p (o b) -> p o b", o=1)
                                .to_broadcast([128, 8, 8]),
                                ALU.add))
                        for he in range(HE):
                            VOP(lambda he=he: V.max(
                                out=v8[:, he * 8:he * 8 + 8],
                                in_=combo[:, he * 64:(he + 1) * 64]))
                        for he in range(HE):
                            hs = slice(he * 8, he * 8 + 8)
                            VOP(lambda he=he: V.tensor_copy(
                                tau8[:, he:he + 1],
                                v8[:, he * 8 + 7:he * 8 + 8]))
                            VOP(lambda he=he, hs=hs: V.tensor_scalar(
                                e8[:, hs], v8[:, hs],
                                v8[:, he * 8:he * 8 + 1], None,
                                op0=ALU.subtract))
                            ACT(e8[:, hs], e8[:, hs], AF.Exp,
                                scale=1.0 / QSC)
                            VOP(lambda he=he, hs=hs: V.reduce_sum(
                                zsum[:, he:he + 1], e8[:, hs], axis=AXX))
                            VOP(lambda he=he: V.tensor_scalar(
                                kt[:, he:he + 1], tau8[:, he:he + 1],
                                v8[:, he * 8:he * 8 + 1], None,
                                op0=ALU.subtract))
                            ACT(kt[:, he:he + 1], kt[:, he:he + 1], AF.Exp,
                                scale=1.0 / QSC)
                            VOP(lambda he=he: V.reciprocal(
                                zsum[:, he:he + 1], zsum[:, he:he + 1]))
                            VOP(lambda he=he: V.tensor_mul(
                                kt[:, he:he + 1], kt[:, he:he + 1],
                                zsum[:, he:he + 1]))
                            VOP(lambda he=he: V.tensor_scalar_mul(
                                taub[:, he:he + 1], tau8[:, he:he + 1],
                                -1.0 / QSC))
                            ACT(exab[:, he * 64:(he + 1) * 64],
                                sxf[:, he * 64:he * 64 + 64], AF.Exp,
                                bias=taub[:, he:he + 1], scale=1.0 / QSC)
                            ACT(exbb[:, he * 64:(he + 1) * 64],
                                sxf[:, 256 + he * 64:256 + he * 64 + 64],
                                AF.Exp, scale=1.0 / QSC)
                            VOP(lambda he=he: V.tensor_tensor(
                                d16[:].rearrange("p (a b) -> p a b", b=64),
                                sxq[:, he * 64:he * 64 + 64].rearrange(
                                    "p (a o) -> p a o", o=1)
                                .to_broadcast([128, 64, 64]),
                                sxq[:, 256 + he * 64:256 + he * 64 + 64]
                                .rearrange("p (o b) -> p o b", o=1)
                                .to_broadcast([128, 64, 64]),
                                ALU.add))
                            VOP(lambda he=he: V.tensor_scalar(
                                mkb[:], d16[:], tau8[:, he:he + 1], None,
                                op0=ALU.is_ge))
                            VOP(lambda he=he: V.tensor_tensor(
                                prodb[:].rearrange("p (a b) -> p a b", b=64),
                                exab[:, he * 64:(he + 1) * 64].rearrange(
                                    "p (a o) -> p a o", o=1)
                                .to_broadcast([128, 64, 64]),
                                exbb[:, he * 64:(he + 1) * 64].rearrange(
                                    "p (o b) -> p o b", o=1)
                                .to_broadcast([128, 64, 64]),
                                ALU.mult))
                            dst = gacc if he == 0 else planeb
                            VOP(lambda he=he, dst=dst:
                                V.scalar_tensor_tensor(
                                    dst[:], prodb[:], kt[:, he:he + 1],
                                    mkb[:], op0=ALU.mult, op1=ALU.mult))
                            if he > 0:
                                VOP(lambda: V.tensor_add(gacc[:], gacc[:],
                                                         planeb[:]))
                        for pp in range(4):
                            for ci in range(8):
                                ec = pp * 8 + ci
                                s.run(T, lambda ec=ec, ci=ci: T.transpose(
                                    ps_tp[:, ci * 128:(ci + 1) * 128],
                                    gacc[:, ec * 128:(ec + 1) * 128],
                                    ident[:]))
                            VOP(lambda: V.tensor_copy(tevac[:], ps_tp[:]))
                            LD(gate_dram.rearrange("(c p) t -> p c t",
                                                   p=128)
                               [:, pp * 8:(pp + 1) * 8,
                                tt * 128:(tt + 1) * 128],
                               tevac[:].rearrange("p (c t) -> p c t", c=8))

                    for ec in range(32):
                        LD(deb[:].rearrange("p (c n) -> p c n", c=8),
                           deT_d.rearrange("(c p) n -> p c n", p=128)
                           [:, :, ec * 128:(ec + 1) * 128], G)
                        for dc in range(8):
                            MM(ps_a[:, 0:CT],
                               deb[:, dc * 128:(dc + 1) * 128],
                               h_mc[:, dc * CT:(dc + 1) * CT],
                               dc == 0, dc == 7)
                        ACT(wsil[:], ps_a[:, 0:CT], AF.Silu)
                        LD(gld[:], gate_dram[ec * 128:(ec + 1) * 128, :])
                        VOP(lambda: V.tensor_mul(wsil[:], wsil[:], gld[:]))
                        LD(a_dram[ec * 128:(ec + 1) * 128, :], wsil[:])

                    banks = [ps_a, ps_b, ps_s0, ps_s1]
                    for half in range(2):
                        for dcd in range(32):
                            LD(wdb[:], wd_d[dcd * 128:(dcd + 1) * 128, :], G)
                            LD(mld[:], m_dram[dcd * 128:(dcd + 1) * 128, :])
                            for di in range(4):
                                dc = half * 4 + di
                                MM(banks[di][:, 0:CT],
                                   wdb[:, dc * 128:(dc + 1) * 128],
                                   mld[:], dcd == 0, False)
                        for ec in range(32):
                            LD(ueb[:], ue_d[ec * 128:(ec + 1) * 128, :], G)
                            LD(mld[:], a_dram[ec * 128:(ec + 1) * 128, :])
                            for di in range(4):
                                dc = half * 4 + di
                                MM(banks[di][:, 0:CT],
                                   ueb[:, dc * 128:(dc + 1) * 128],
                                   mld[:], False, ec == 31)
                        for di in range(4):
                            dc = half * 4 + di
                            VOP(lambda dc=dc, di=di: V.scalar_tensor_tensor(
                                xc[:, dc * CT:(dc + 1) * CT],
                                banks[di][:, 0:CT], bd_sb[:, dc:dc + 1],
                                x2[:, dc * CT:(dc + 1) * CT],
                                op0=ALU.add, op1=ALU.add))
                    LD(out_d.rearrange("m (c p) t -> m p c t", p=128)[mci],
                       xc[:].rearrange("p (c t) -> p c t", c=8))

            SY.wait_ge(gsem, s.count)

    return nc


_NC_CACHE = None
_LAST_RESULTS = None


def kernel(L0_attn_x, L0_attn_Wqkv, L0_attn_Wo, L0_attn_log_xi,
           L0_attn_pi_gate_logit, L0_attn_e_gate_logit, L0_ffn_Wg, L0_ffn_bg,
           L0_ffn_Wu, L0_ffn_bu, L0_ffn_Wd, L0_ffn_bd, L0_ffn_Wq, L0_ffn_keys,
           L0_ffn_down_embed, L0_ffn_up_embed):
    global _NC_CACHE

    x = np.asarray(L0_attn_x, np.float32)
    Wqkv = np.asarray(L0_attn_Wqkv, np.float32)

    perm = np.empty(3 * D, np.int64)
    for g in range(4):
        for half in range(2):
            for hh in range(4):
                for f in range(32):
                    c = g * 256 + half * 128 + hh * 32 + f
                    row = (4 * g + hh) * 64 + 2 * f + half
                    perm[c] = row
                    perm[D + c] = D + row
    perm[2 * D:] = np.arange(2 * D, 3 * D)
    wqkvT = np.ascontiguousarray(Wqkv[perm].T)

    woT = np.asarray(L0_attn_Wo, np.float32).T
    wo_r = np.ascontiguousarray(
        woT.reshape(D, 8, 128).transpose(1, 0, 2))
    gates3 = np.ascontiguousarray(np.stack(
        [np.asarray(L0_attn_log_xi, np.float32),
         np.asarray(L0_attn_pi_gate_logit, np.float32),
         np.asarray(L0_attn_e_gate_logit, np.float32)], axis=1))

    wg = np.ascontiguousarray(np.asarray(L0_ffn_Wg, np.float32))
    wu = np.ascontiguousarray(np.asarray(L0_ffn_Wu, np.float32))
    wd = np.ascontiguousarray(np.asarray(L0_ffn_Wd, np.float32))
    bg2 = np.ascontiguousarray(
        np.asarray(L0_ffn_bg, np.float32).reshape(32, 128).T)
    bu2 = np.ascontiguousarray(
        np.asarray(L0_ffn_bu, np.float32).reshape(32, 128).T)
    bd2 = np.ascontiguousarray(
        np.asarray(L0_ffn_bd, np.float32).reshape(8, 128).T)
    wq = np.asarray(L0_ffn_Wq, np.float32)
    wq_r = np.ascontiguousarray(
        wq.reshape(D, 4, 128).transpose(1, 0, 2))
    keys = np.asarray(L0_ffn_keys, np.float32)
    kbd = np.zeros((512, 512), np.float32)
    for p in range(2):
        for he in range(HE):
            r0 = p * 256 + he * 64
            kbd[r0:r0 + 64, r0:r0 + 64] = keys[he, :, p, :].T
    deT = np.ascontiguousarray(np.asarray(L0_ffn_down_embed, np.float32).T)
    ue = np.ascontiguousarray(np.asarray(L0_ffn_up_embed, np.float32))

    import jax
    import jax.numpy as jnp
    with jax.default_device(jax.local_devices(backend="cpu")[0]):
        kfrac_j = jnp.arange(0, 64, 2, dtype=jnp.float32) / 64
        pi_inv_j = jnp.float32(math.pi) ** (1.0 - kfrac_j)
        pos_j = jnp.arange(S, dtype=jnp.float32)
        th_j = pos_j[:, None] * pi_inv_j[None, :]
        th_np = np.asarray(th_j)
    th0 = np.ascontiguousarray(
        np.tile(th_np.T, (4, 1)).astype(np.float32))

    kpos = np.ascontiguousarray(
        (np.arange(128, dtype=np.float32)[:, None]
         + 128.0 * np.arange(64, dtype=np.float32)[None, :]))

    rselm = np.zeros((16, 4 * 128), np.float32)
    for g in range(4):
        for r in range(128):
            rselm[4 * g + r // 32, g * 128 + r] = 1.0

    cmask = np.zeros((128, 4 * CT), ml_dtypes.bfloat16)
    for dv in range(4):
        for p in range(128):
            cmask[p, dv * CT + dv * 128 + p:(dv + 1) * CT] = 1.0

    if _NC_CACHE is None:
        _NC_CACHE = build_nc()
    nc = _NC_CACHE

    shared = dict(
        wqkvT=wqkvT, wo_r=wo_r, gates3=gates3, wg_d=wg, wu_d=wu, wd_d=wd,
        bg_d=bg2, bu_d=bu2, bd_d=bd2, wq_r=wq_r, kbd_d=kbd, deT_d=deT,
        ue_d=ue, th0_d=th0, kpos_d=kpos, rsel_d=rselm, cmask_d=cmask,
        ident_d=np.eye(128, dtype=ml_dtypes.bfloat16),
    )
    in_maps = []
    for core in range(8):
        b, bc = core // 4, core % 4
        qp = np.concatenate(
            [np.arange(mc * CT, (mc + 1) * CT, dtype=np.float32)
             for mc in CHUNKS[bc]])
        in_maps.append(dict(
            shared,
            xT=np.ascontiguousarray(x[b].T),
            qpos_d=np.ascontiguousarray(np.tile(qp[None, :], (16, 1))),
        ))

    res = run_bass_kernel_spmd(nc, in_maps, core_ids=list(range(8)))
    global _LAST_RESULTS
    _LAST_RESULTS = res

    out = np.empty((B, S, D), np.float32)
    for core in range(8):
        b, bc = core // 4, core % 4
        o = res.results[core]["out"]
        for mci, mc in enumerate(CHUNKS[bc]):
            out[b, mc * CT:(mc + 1) * CT, :] = o[mci].T
    return out


# version 15
# speedup vs baseline: 2.4805x; 2.4805x over previous
import math
from contextlib import ExitStack

import numpy as np
import ml_dtypes

import concourse.bass as bass
import concourse.mybir as mybir
from concourse.bass_utils import run_bass_kernel_spmd

F32 = mybir.dt.float32
BF16 = mybir.dt.bfloat16
I16 = mybir.dt.int16
I32 = mybir.dt.int32
AF = mybir.ActivationFunctionType
ALU = mybir.AluOpType
AXX = mybir.AxisListType.X

B, S, D = 2, 8192, 1024
H = 16
NC, CT = 16, 512
HE = 4
QSC = 4096.0
INV8 = 0.125
SHIFT = 8.0
TWO_PI = 2.0 * math.pi
INV2PI = float(np.float32(1.0 / (2.0 * math.pi)))
_bb = np.float32(INV2PI).view(np.int32) & np.int32(-4096)
C_HI = float(np.int32(_bb).view(np.float32))
C_LO = float(np.float32(1.0 / (2.0 * math.pi) - C_HI))

CHUNKS = [sorted([bc, 7 - bc, 8 + bc, 15 - bc]) for bc in range(4)]
DEBUG = False
SERIAL = False


def _bufname(x):
    if isinstance(x, str):
        return x
    n = getattr(x, "name", None)
    if n is not None:
        return n
    return str(x)


class Pipe:

    def __init__(self, engs, sems):
        self.engs = engs
        self.sems = sems
        self.cum = {e: 0 for e in engs}
        self.last_w = {}
        self.last_r = {}
        self.synced = {}

    def op(self, ename, fn, outs, ins, inc=1):
        eng = self.engs[ename]
        targets = {}

        def need(e, v):
            if e == "T" and ename == "T":
                return
            if v > targets.get(e, 0):
                targets[e] = v

        if SERIAL:
            for e in self.engs:
                if self.cum[e]:
                    targets[e] = self.cum[e]
        inames = [_bufname(b) for b in ins]
        onames = [_bufname(b) for b in outs]
        for b in inames:
            w = self.last_w.get(b)
            if w:
                need(*w)
        for b in onames:
            w = self.last_w.get(b)
            if w:
                need(*w)
            for e, v in self.last_r.get(b, {}).items():
                need(e, v)
        for t in list(targets):
            if t in ("SY", "G"):
                targets[t] = self.cum[t]
        for t, val in sorted(targets.items()):
            if self.synced.get((ename, t), 0) < val:
                eng.wait_ge(self.sems[t], val)
                self.synced[(ename, t)] = val
        inst = fn()
        inst.then_inc(self.sems[ename], inc)
        self.cum[ename] += inc
        cv = self.cum[ename]
        for b in inames:
            self.last_r.setdefault(b, {})[ename] = cv
        for b in onames:
            self.last_w[b] = (ename, cv)
            self.last_r[b] = {}
        return inst

    def barrier(self):
        for e in self.engs:
            eng = self.engs[e]
            for t in self.engs:
                if self.cum[t] and \
                        self.synced.get((e, t), 0) < self.cum[t]:
                    eng.wait_ge(self.sems[t], self.cum[t])
                    self.synced[(e, t)] = self.cum[t]

    def finish(self, ename):
        eng = self.engs[ename]
        for t in self.engs:
            if self.cum[t] and self.synced.get((ename, t), 0) < self.cum[t]:
                eng.wait_ge(self.sems[t], self.cum[t])
                self.synced[(ename, t)] = self.cum[t]


def build_nc():
    nc = bass.Bass("TRN2", target_bir_lowering=False)

    ap = lambda t: t.ap()
    xT = ap(nc.dram_tensor("xT", [D, S], F32, kind="ExternalInput"))
    wqkvT = ap(nc.dram_tensor("wqkvT", [D, 3 * D], F32, kind="ExternalInput"))
    wo_r = ap(nc.dram_tensor("wo_r", [8, D, 128], F32, kind="ExternalInput"))
    gates3 = ap(nc.dram_tensor("gates3", [16, 3], F32, kind="ExternalInput"))
    wg_d = ap(nc.dram_tensor("wg_d", [D, 4096], F32, kind="ExternalInput"))
    wu_d = ap(nc.dram_tensor("wu_d", [D, 4096], F32, kind="ExternalInput"))
    wd_d = ap(nc.dram_tensor("wd_d", [4096, D], F32, kind="ExternalInput"))
    bg_d = ap(nc.dram_tensor("bg_d", [128, 32], F32, kind="ExternalInput"))
    bu_d = ap(nc.dram_tensor("bu_d", [128, 32], F32, kind="ExternalInput"))
    bd_d = ap(nc.dram_tensor("bd_d", [128, 8], F32, kind="ExternalInput"))
    wq_r = ap(nc.dram_tensor("wq_r", [4, D, 128], F32, kind="ExternalInput"))
    kbd_d = ap(nc.dram_tensor("kbd_d", [512, 512], F32, kind="ExternalInput"))
    deT_d = ap(nc.dram_tensor("deT_d", [D, 4096], F32, kind="ExternalInput"))
    ue_d = ap(nc.dram_tensor("ue_d", [4096, D], F32, kind="ExternalInput"))
    th0_d = ap(nc.dram_tensor("th0_d", [128, S], F32, kind="ExternalInput"))
    qpos_d = ap(nc.dram_tensor("qpos_d", [16, 4 * CT], F32, kind="ExternalInput"))
    kpos_d = ap(nc.dram_tensor("kpos_d", [128, 64], F32, kind="ExternalInput"))
    rsel_d = ap(nc.dram_tensor("rsel_d", [16, 4 * 128], F32, kind="ExternalInput"))
    cmask_d = ap(nc.dram_tensor("cmask_d", [128, 4 * CT], BF16, kind="ExternalInput"))
    ident_d = ap(nc.dram_tensor("ident_d", [128, 128], BF16, kind="ExternalInput"))

    out_d = ap(nc.dram_tensor("out", [4, D, CT], F32, kind="ExternalOutput"))

    kind = "ExternalOutput" if DEBUG else "Internal"
    k_dram = ap(nc.dram_tensor("k_dram", [D, S], BF16, kind=kind))
    v_dram = ap(nc.dram_tensor("v_dram", [4, 64, 128, 4 * 65], BF16, kind=kind))
    q_dram = ap(nc.dram_tensor("q_dram", [D, 4 * CT], BF16, kind=kind))
    g_dram = ap(nc.dram_tensor("g_dram", [3, 16, 4 * CT], BF16, kind=kind))
    srow_dram = ap(nc.dram_tensor("srow_dram", [2, 16], F32))
    gate_dram = ap(nc.dram_tensor("gate_dram", [4096, CT], BF16, kind=kind))
    a_dram = ap(nc.dram_tensor("a_dram", [4096, CT], BF16, kind=kind))
    m_dram = ap(nc.dram_tensor("m_dram", [4096, CT], BF16, kind=kind))
    dbg_x2 = ap(nc.dram_tensor("dbg_x2", [4, D, CT], F32, kind=kind)) \
        if DEBUG else None

    with ExitStack() as st0:
        def mk(stack, sfx=""):
            def sb(name, shape, dtype):
                return stack.enter_context(
                    nc.sbuf_tensor(name + sfx, shape, dtype))
            return sb

        sb0 = mk(st0)
        ones1f = sb0("ones1f", [1, 128], F32)
        onesb = sb0("onesb", [128, 1], BF16)
        eps_t = sb0("eps_t", [1, 1], F32)
        quart_t = sb0("quart_t", [128, 1], F32)
        ident = sb0("ident", [128, 128], BF16)
        gsig = sb0("gsig", [16, 3], F32)
        xi_r = sb0("xi_r", [16, 1], F32)
        pig = sb0("pig", [16, 1], F32)
        egf = sb0("egf", [16, 1], F32)
        chh = sb0("chh", [16, 1], F32)
        cm8 = sb0("cm8", [16, 1], F32)
        tmp16 = sb0("tmp16", [16, 4], F32)
        pigcol = sb0("pigcol", [128, 4], F32)
        crep = sb0("crep", [128, 16], F32)
        kc = sb0("kc", [128, 64 * 16], F32)
        cmask = sb0("cmask", [128, 4 * CT], BF16)
        rows = sb0("rows", [1, 6 * CT], F32)
        rb_sb = sb0("rb_sb", [128, CT], F32)
        mrb_sb = sb0("mrb_sb", [128, CT], F32)
        xb1 = [sb0("xb1" + str(i), [128, CT], BF16) for i in range(2)]
        sq1 = [sb0("sq1" + str(i), [128, CT], BF16) for i in range(2)]
        xc = sb0("xc", [128, 8 * CT], F32)

        ps_a = st0.enter_context(nc.psum_tensor("ps_a", [128, 512], F32))
        ps_b = st0.enter_context(nc.psum_tensor("ps_b", [128, 512], F32))
        ps_s0 = st0.enter_context(nc.psum_tensor("ps_s0", [128, 512], F32))
        ps_s1 = st0.enter_context(nc.psum_tensor("ps_s1", [128, 512], F32))
        ps_c0 = st0.enter_context(nc.psum_tensor("ps_c0", [128, 512], F32))
        ps_c1 = st0.enter_context(nc.psum_tensor("ps_c1", [128, 512], F32))
        ps_tp = st0.enter_context(nc.psum_tensor("ps_tp", [128, 512], F32))
        ps_g = st0.enter_context(nc.psum_tensor("ps_g", [128, 512], F32))

        sems = {n: st0.enter_context(nc.semaphore("sem_" + n))
                for n in ("V", "SC", "T", "G", "SY")}
        blk = st0.enter_context(nc.Block())

        V, SC, T, G, SY = nc.vector, nc.scalar, nc.tensor, nc.gpsimd, nc.sync
        ENGS = {"V": V, "SC": SC, "T": T, "G": G, "SY": SY}

        M_ROW = rows[:, 0:CT]
        V_ROW = rows[:, CT:2 * CT]
        R_ROW = rows[:, 2 * CT:3 * CT]
        MR_ROW = rows[:, 3 * CT:4 * CT]
        Z_ROW = rows[:, 4 * CT:5 * CT]

        engs_l = [V, SC, T, G, SY]
        idxs = [e.compute_val(e.partition_id() % 4) for e in engs_l]

        for bc in nc.Switch(engines=engs_l, index=idxs, n=4):
            p = Pipe(ENGS, sems)
            ch_list = CHUNKS[bc]

            def LD(dst, src, eng="SY", **kw):
                p.op(eng, lambda: ENGS[eng].dma_start(dst, src, **kw),
                     outs=[dst], ins=[src], inc=16)

            def MM(out, lhsT, rhs, start, stop):
                ins = [lhsT, rhs] + ([] if start else [out])
                p.op("T", lambda: T.matmul(out, lhsT, rhs, start=start,
                                           stop=stop),
                     outs=[out], ins=ins)

            def DV(fn, outs, ins):
                p.op("V", fn, outs=outs, ins=ins)

            def ACT(out, in_, func, bias=0.0, scale=1.0):
                ins = [in_] + ([bias] if isinstance(bias, bass.AP) else [])
                p.op("SC", lambda: SC.activation(out, in_, func, bias=bias,
                                                 scale=scale),
                     outs=[out], ins=ins)

            DV(lambda: V.memset(ones1f[:], 1.0), [ones1f], [])
            DV(lambda: V.memset(onesb[:], 1.0), [onesb], [])
            DV(lambda: V.memset(eps_t[:], 1e-5), [eps_t], [])
            DV(lambda: V.memset(quart_t[:], 0.25), [quart_t], [])
            LD(ident[:], ident_d)
            LD(gsig[:], gates3)
            LD(rb_sb[:, 0:64], kpos_d)
            LD(cmask[:], cmask_d)
            ACT(xi_r[:], gsig[:, 0:1], AF.Exp)
            DV(lambda: V.reciprocal(xi_r[:], xi_r[:]), [xi_r], [xi_r])
            for col, dst in ((1, pig), (2, egf)):
                ACT(dst[:], gsig[:, col:col + 1], AF.Exp, scale=-1.0)
                DV(lambda d=dst: V.tensor_scalar_add(d[:], d[:], 1.0),
                   [dst], [dst])
                DV(lambda d=dst: V.reciprocal(d[:], d[:]), [dst], [dst])
            DV(lambda: V.tensor_mul(chh[:], egf[:], xi_r[:]),
               [chh], [egf, xi_r])
            DV(lambda: V.tensor_scalar_mul(cm8[:], chh[:], -8.0),
               [cm8], [chh])
            DV(lambda: V.tensor_copy(tmp16[:, 0:1], cm8[:]), [tmp16], [cm8])
            DV(lambda: V.tensor_copy(tmp16[:, 1:2], chh[:]), [tmp16], [chh])
            LD(srow_dram.rearrange("a p -> p a"), tmp16[:, 0:2],
               allow_slow_non_contiguous=True)
            LD(crep[0:1, 0:16], srow_dram[1:2, :])
            MM(ps_a[:, 0:16], ones1f[:], crep[0:1, 0:16], True, True)
            DV(lambda: V.tensor_copy(crep[:], ps_a[:, 0:16]), [crep], [ps_a])
            DV(lambda: V.tensor_tensor(
                kc[:].rearrange("p (k h) -> p k h", h=16),
                rb_sb[:, 0:64].rearrange("p (k o) -> p k o", o=1)
                .to_broadcast([128, 64, 16]),
                crep[:].rearrange("p (o h) -> p o h", o=1)
                .to_broadcast([128, 64, 16]),
                ALU.mult), [kc], [rb_sb, crep])
            DV(lambda: V.tensor_scalar_add(kc[:], kc[:], -SHIFT), [kc], [kc])
            DV(lambda: V.tensor_copy(tmp16[:, 2:3], pig[:]), [tmp16], [pig])
            with ExitStack() as st1:
                sb1 = mk(st1, f"_{bc}")
                rsel = sb1("rsel", [16, 4 * 128], F32)
                qpos = sb1("qpos", [16, 4 * CT], F32)
                g_pm = sb1("g_pm", [16, 3 * 4 * CT], BF16)
                gt1 = sb1("gt1", [16, 4 * CT], F32)
                gt2 = sb1("gt2", [16, 4 * CT], F32)
                LD(rsel[:], rsel_d)
                LD(qpos[:], qpos_d)
                for g in range(4):
                    MM(ps_a[:, g:g + 1], rsel[:, g * 128:(g + 1) * 128],
                       tmp16[:, 2:3], True, True)
                DV(lambda: V.tensor_copy(pigcol[:], ps_a[:, 0:4]),
                   [pigcol], [ps_a])
                DV(lambda: V.tensor_scalar(gt1[:], qpos[:], cm8[:, 0:1],
                                           None, op0=ALU.mult),
                   [gt1], [qpos, cm8])
                DV(lambda: V.tensor_copy(g_pm[:, 0:2048], gt1[:]),
                   [g_pm], [gt1])
                DV(lambda: V.tensor_sub(gt2[:], gt1[:], g_pm[:, 0:2048]),
                   [gt2], [gt1, g_pm])
                DV(lambda: V.tensor_copy(g_pm[:, 2048:4096], gt2[:]),
                   [g_pm], [gt2])
                DV(lambda: V.tensor_sub(gt1[:], gt2[:], g_pm[:, 2048:4096]),
                   [gt1], [gt2, g_pm])
                DV(lambda: V.tensor_copy(g_pm[:, 4096:6144], gt1[:]),
                   [g_pm], [gt1])
                LD(g_dram.rearrange("r h t -> h r t"),
                   g_pm[:].rearrange("h (r t) -> h r t", r=3))

            def layernorm(src, out_bf):
                for dc in range(8):
                    sl = slice(dc * CT, (dc + 1) * CT)
                    xb = xb1[dc % 2]
                    sq = sq1[dc % 2]
                    ACT(xb[:], src[:, sl], AF.Identity)
                    DV(lambda xb=xb, sq=sq: V.tensor_mul(sq[:], xb[:],
                                                         xb[:]),
                       [sq], [xb])
                    MM(ps_s0[0:1, 0:CT], onesb[:], xb[:], dc == 0, dc == 7)
                    MM(ps_s1[0:1, 0:CT], onesb[:], sq[:], dc == 0, dc == 7)
                DV(lambda: V.tensor_scalar_mul(M_ROW, ps_s0[0:1, 0:CT],
                                               1.0 / D), [rows], [ps_s0])
                DV(lambda: V.tensor_scalar_mul(V_ROW, ps_s1[0:1, 0:CT],
                                               1.0 / D), [rows], [ps_s1])
                DV(lambda: V.tensor_mul(Z_ROW, M_ROW, M_ROW), [rows], [rows])
                DV(lambda: V.tensor_sub(V_ROW, V_ROW, Z_ROW), [rows], [rows])
                ACT(V_ROW, V_ROW, AF.Ln, bias=eps_t[0:1, 0:1])
                ACT(R_ROW, V_ROW, AF.Exp, scale=-0.5)
                DV(lambda: V.tensor_mul(MR_ROW, M_ROW, R_ROW), [rows], [rows])
                DV(lambda: V.tensor_scalar_mul(MR_ROW, MR_ROW, -1.0),
                   [rows], [rows])
                MM(ps_a[:, 0:CT], ones1f[:], R_ROW, True, True)
                MM(ps_b[:, 0:CT], ones1f[:], MR_ROW, True, True)
                DV(lambda: V.tensor_copy(rb_sb[:], ps_a[:, 0:CT]),
                   [rb_sb], [ps_a])
                DV(lambda: V.tensor_copy(mrb_sb[:], ps_b[:, 0:CT]),
                   [mrb_sb], [ps_b])
                for dc in range(8):
                    sl = slice(dc * CT, (dc + 1) * CT)
                    DV(lambda sl=sl: V.tensor_tensor(
                        out_bf[:, sl], src[:, sl], rb_sb[:], ALU.mult),
                       [out_bf], [src, rb_sb])
                    DV(lambda sl=sl: V.tensor_tensor(
                        out_bf[:, sl], out_bf[:, sl], mrb_sb[:], ALU.add),
                       [out_bf], [out_bf, mrb_sb])

            p.barrier()

            with ExitStack() as st2:
                sb2 = mk(st2, f"_{bc}")
                wqkv_sb = sb2("wqkv_sb", [128, 8 * 3072], BF16)
                lnb = sb2("lnb", [128, 8 * CT], BF16)
                thf = sb2("thf", [128, 4 * CT], F32)
                rnd32 = sb2("rnd32", [128, 4 * CT], I32)
                frac = sb2("frac", [128, 4 * CT], F32)
                sc2 = sb2("sc2", [128, 4 * CT], F32)
                sc3 = sb2("sc3", [128, 4 * CT], F32)
                csnb = sb2("csnb", [128, 4 * CT], BF16)
                ccsb = sb2("ccsb", [128, 4 * CT], BF16)
                kqA = sb2("kqA", [128, CT], BF16)
                kqB = sb2("kqB", [128, CT], BF16)
                rotA = [sb2(f"rotA{i}", [128, CT], BF16) for i in range(2)]
                rotB = [sb2(f"rotB{i}", [128, CT], BF16) for i in range(2)]
                rtmp = sb2("rtmp", [128, CT], BF16)
                v_loc = [sb2(f"v_loc{i}", [128, 4 * 16 * 65], BF16)
                         for i in range(2)]

                LD(wqkv_sb[:].rearrange("p (c n) -> p c n", c=8),
                   wqkvT.rearrange("(c p) n -> p c n", p=128), "G")

                for tc in range(NC):
                    mine = tc in ch_list
                    mci = ch_list.index(tc) if mine else -1

                    LD(sc3[:, 0:CT], th0_d[:, tc * CT:(tc + 1) * CT])
                    DV(lambda: V.tensor_tensor(
                        thf[:].rearrange("p (g t) -> p g t", g=4),
                        sc3[:, 0:CT].rearrange("p (o t) -> p o t", o=1)
                        .to_broadcast([128, 4, CT]),
                        pigcol[:].rearrange("p (g o) -> p g o", o=1)
                        .to_broadcast([128, 4, CT]),
                        ALU.mult), [thf], [sc3, pigcol])
                    DV(lambda: V.tensor_single_scalar(
                        sc2[:].bitcast(I32), thf[:].bitcast(I32), -4096,
                        ALU.bitwise_and), [sc2], [thf])
                    DV(lambda: V.tensor_sub(sc3[:], thf[:], sc2[:]),
                       [sc3], [thf, sc2])
                    ACT(thf[:], sc2[:], AF.Identity, scale=C_HI)
                    DV(lambda: V.tensor_copy(rnd32[:], thf[:]),
                       [rnd32], [thf])
                    DV(lambda: V.tensor_copy(frac[:], rnd32[:]),
                       [frac], [rnd32])
                    DV(lambda: V.tensor_sub(thf[:], thf[:], frac[:]),
                       [thf], [thf, frac])
                    ACT(sc2[:], sc2[:], AF.Identity, scale=C_LO)
                    ACT(sc3[:], sc3[:], AF.Identity, scale=INV2PI)
                    DV(lambda: V.tensor_add(thf[:], thf[:], sc2[:]),
                       [thf], [thf, sc2])
                    DV(lambda: V.tensor_add(thf[:], thf[:], sc3[:]),
                       [thf], [thf, sc3])
                    DV(lambda: V.tensor_copy(rnd32[:], thf[:]),
                       [rnd32], [thf])
                    DV(lambda: V.tensor_copy(frac[:], rnd32[:]),
                       [frac], [rnd32])
                    DV(lambda: V.tensor_sub(frac[:], thf[:], frac[:]),
                       [frac], [thf, frac])
                    ACT(csnb[:], frac[:], AF.Sin, scale=TWO_PI)
                    ACT(frac[:], frac[:], AF.Identity,
                        bias=quart_t[:, 0:1])
                    DV(lambda: V.tensor_scalar(thf[:], frac[:], 0.5, None,
                                               op0=ALU.is_gt), [thf], [frac])
                    DV(lambda: V.tensor_sub(frac[:], frac[:], thf[:]),
                       [frac], [frac, thf])
                    ACT(ccsb[:], frac[:], AF.Sin, scale=TWO_PI)

                    LD(xc[:].rearrange("p (c t) -> p c t", c=8),
                       xT.rearrange("(c p) t -> p c t", p=128)
                       [:, :, tc * CT:(tc + 1) * CT])
                    layernorm(xc[:], lnb)

                    def proj_rot(col_base, dst_dram, dst_col):
                        for g in range(4):
                            rA = rotA[g % 2]
                            rB = rotB[g % 2]
                            for half, dst_ps in ((0, ps_a), (1, ps_b)):
                                cb = col_base + g * 256 + half * 128
                                for dc in range(8):
                                    MM(dst_ps[:, 0:CT],
                                       wqkv_sb[:, dc * 3072 + cb:
                                               dc * 3072 + cb + 128],
                                       lnb[:, dc * CT:(dc + 1) * CT],
                                       dc == 0, dc == 7)
                            DV(lambda: V.tensor_copy(kqA[:], ps_a[:, 0:CT]),
                               [kqA], [ps_a])
                            DV(lambda: V.tensor_copy(kqB[:], ps_b[:, 0:CT]),
                               [kqB], [ps_b])
                            gs = slice(g * CT, (g + 1) * CT)
                            DV(lambda gs=gs, rA=rA: V.tensor_tensor(
                                rA[:], kqA[:], ccsb[:, gs], ALU.mult),
                               [rA], [kqA, ccsb])
                            DV(lambda gs=gs: V.tensor_tensor(
                                rtmp[:], kqB[:], csnb[:, gs], ALU.mult),
                               [rtmp], [kqB, csnb])
                            DV(lambda rA=rA: V.tensor_sub(rA[:], rA[:],
                                                          rtmp[:]),
                               [rA], [rA, rtmp])
                            DV(lambda gs=gs, rB=rB: V.tensor_tensor(
                                rB[:], kqA[:], csnb[:, gs], ALU.mult),
                               [rB], [kqA, csnb])
                            DV(lambda gs=gs: V.tensor_tensor(
                                rtmp[:], kqB[:], ccsb[:, gs], ALU.mult),
                               [rtmp], [kqB, ccsb])
                            DV(lambda rB=rB: V.tensor_add(rB[:], rB[:],
                                                          rtmp[:]),
                               [rB], [rB, rtmp])
                            for half, rsrc in ((0, rA), (1, rB)):
                                LD(dst_dram.rearrange(
                                    "(hh r) t -> hh r t", r=64)
                                   [g * 4:(g + 1) * 4,
                                    half * 32:half * 32 + 32,
                                    dst_col:dst_col + CT],
                                   rsrc[:])

                    proj_rot(1024, k_dram, tc * CT)
                    if mine:
                        proj_rot(0, q_dram, mci * CT)

                    vl = v_loc[tc % 2]
                    for kcq in range(4):
                        for half, dst_ps in ((0, ps_a), (1, ps_b)):
                            for dc in range(8):
                                MM(dst_ps[:, 0:512],
                                   lnb[:, dc * CT + kcq * 128:
                                       dc * CT + kcq * 128 + 128],
                                   wqkv_sb[:, dc * 3072 + 2048 + half * 512:
                                           dc * 3072 + 2048 + half * 512
                                           + 512],
                                   dc == 0, dc == 7)
                        for half, src_ps in ((0, ps_a), (1, ps_b)):
                            DV(lambda kcq=kcq, half=half, src_ps=src_ps,
                               vl=vl: V.tensor_copy(
                                   bass.AP(vl,
                                           kcq * 1040 + half * 520,
                                           [[4160, 128], [65, 8], [1, 64]]),
                                   src_ps[:, 0:512].rearrange(
                                       "p (h d) -> p h d", h=8)),
                               [vl], [src_ps])
                    DV(lambda vl=vl: V.memset(
                        bass.AP(vl, 64,
                                [[4160, 128], [65, 64], [1, 1]]), 1.0),
                       [vl], [])
                    for g in range(4):
                        LD(v_dram[g].rearrange("k p c -> p k c")
                           [:, tc * 4:(tc + 1) * 4, :],
                           vl[:].rearrange("p (k g c) -> p k g c",
                                           k=4, g=4)[:, :, g, :])

            p.barrier()

            with ExitStack() as st3:
                sb3 = mk(st3, f"_{bc}")
                def dbl(name, shape, dtype):
                    return [sb3(name + str(i), shape, dtype)
                            for i in range(2)]
                wob = dbl("wob", [128, 8 * 128], BF16)
                wqb = dbl("wqb", [128, 8 * 128], BF16)
                kbd_sb = sb3("kbd_sb", [128, 4 * 512], BF16)
                bg_sb = sb3("bg_sb", [128, 32], F32)
                bu_sb = sb3("bu_sb", [128, 32], F32)
                bd_sb = sb3("bd_sb", [128, 8], F32)
                q_sub = dbl("q_sub", [67, CT], BF16)
                k_sub = dbl("k_sub", [67, 1024], BF16)
                v_sub = dbl("v_sub", [128, 8 * 260], BF16)
                p_t = [sb3("p_t" + str(i), [128, CT], BF16)
                       for i in range(3)]
                rb64 = sb3("rb64", [64, CT], F32)
                o_fm = sb3("o_fm", [128, 8 * CT], BF16)
                x2 = sb3("x2", [128, 8 * CT], F32)
                h_mc = sb3("h_mc", [128, 8 * CT], BF16)
                wgu = dbl("wgu", [128, 2 * 1024], BF16)
                wdb = dbl("wdb", [128, 1024], BF16)
                gch = dbl("gch", [128, CT], BF16)
                uch = dbl("uch", [128, CT], BF16)
                qe_fm = sb3("qe_fm", [128, 4 * CT], BF16)
                sim_f = sb3("sim_f", [128, 512], F32)
                sxq_t = [sb3(f"sxq{i}", [128, 512], I16) for i in range(4)]
                sxf = sb3("sxf", [128, 512], F32)
                v8 = sb3("v8", [128, 8 * HE], F32)
                vy8 = sb3("vy8", [128, 8 * HE], F32)
                combo = sb3("combo", [128, 64 * HE], F32)
                tau8 = sb3("tau8", [128, HE], F32)
                taub = sb3("taub", [128, HE], F32)
                e8 = sb3("e8", [128, 8 * HE], F32)
                zsum = sb3("zsum", [128, HE], F32)
                kt = sb3("kt", [128, HE], F32)
                exab = sb3("exab", [128, 64 * HE], BF16)
                exbb = sb3("exbb", [128, 64 * HE], BF16)
                d16 = sb3("d16", [128, 4096], I16)
                mkb = sb3("mkb", [128, 4096], BF16)
                prodb = sb3("prodb", [128, 4096], BF16)
                gacc = dbl("gacc", [128, 4096], BF16)
                tevac = dbl("tevac", [128, 512], BF16)
                ueb = dbl("ueb", [128, 1024], BF16)
                deb = dbl("deb", [128, 8 * 128], BF16)
                wsil = dbl("wsil", [128, CT], BF16)
                gld = dbl("gld", [128, CT], BF16)
                mld = dbl("mld", [128, CT], BF16)

                DV(lambda: V.memset(k_sub[0][64:67, :], 1.0), [k_sub[0]], [])
                DV(lambda: V.memset(k_sub[1][64:67, :], 1.0), [k_sub[1]], [])

                LD(kbd_sb[:].rearrange("p (c n) -> p c n", c=4),
                   kbd_d.rearrange("(c p) n -> p c n", p=128), "G")
                LD(bg_sb[:], bg_d)
                LD(bu_sb[:], bu_d)
                LD(bd_sb[:], bd_d)

                for mci in range(4):
                    mc = ch_list[mci]
                    nkc = (mc + 1) * 4
                    for h in range(H):
                        grp = h // 4
                        hl = h % 4
                        qb = q_sub[h % 2]
                        acc = [ps_c0, ps_c1][h % 2]
                        LD(qb[0:64, :],
                           q_dram[h * 64:(h + 1) * 64,
                                  mci * CT:(mci + 1) * CT])
                        LD(qb[64:67, :],
                           g_dram[:, h, mci * CT:(mci + 1) * CT])
                        nks = (nkc + 7) // 8
                        pend = None

                        def do_av(acc, hl, nkc):
                            pkb, ppt, pvsl, pkb0 = pend
                            p.op("T", lambda: T.matmul(
                                acc[0:65, 0:CT],
                                pvsl[:, (pkb - pkb0) * 260 + hl * 65:
                                     (pkb - pkb0) * 260 + hl * 65 + 65],
                                ppt[:], start=(pkb == 0),
                                stop=(pkb == nkc - 1)),
                                 outs=[acc],
                                 ins=[pvsl, ppt]
                                 + ([] if pkb == 0 else [acc]))

                        for ks in range(nks):
                            kb0 = ks * 8
                            kb1 = min(nkc, kb0 + 8)
                            kpar = (h * nks + ks) % 2
                            ksl = k_sub[kpar]
                            vsl = v_sub[kpar]
                            p.op("SY", lambda ksl=ksl, kb0=kb0, kb1=kb1,
                                 h=h: SY.dma_start(
                                     ksl[0:64, 0:(kb1 - kb0) * 128],
                                     k_dram[h * 64:(h + 1) * 64,
                                            kb0 * 128:kb1 * 128]),
                                 outs=[ksl], ins=[k_dram], inc=16)
                            p.op("SY", lambda vsl=vsl, kb0=kb0, kb1=kb1,
                                 grp=grp: SY.dma_start(
                                     vsl[:].rearrange("p (k c) -> p k c",
                                                      c=260)
                                     [:, 0:kb1 - kb0, :],
                                     v_dram[grp, kb0:kb1, :, :].rearrange(
                                         "k p c -> p k c")),
                                 outs=[vsl], ins=[v_dram], inc=16)
                            for kb in range(kb0, kb1):
                                sps = [ps_s0, ps_s1][kb % 2]
                                pt = p_t[kb % 3]
                                p.op("T", lambda sps=sps, ksl=ksl, kb=kb,
                                     kb0=kb0, qb=qb: T.matmul(
                                         sps[:, 0:CT],
                                         ksl[0:67, (kb - kb0) * 128:
                                             (kb - kb0 + 1) * 128],
                                         qb[0:67, :], start=True,
                                         stop=True),
                                     outs=[sps], ins=[ksl, qb])
                                p.op("SC", lambda pt=pt, sps=sps, kb=kb,
                                     h=h: SC.activation(
                                         pt[:], sps[:, 0:CT], AF.Exp,
                                         bias=kc[:, kb * 16 + h:
                                                 kb * 16 + h + 1],
                                         scale=INV8),
                                     outs=[pt], ins=[sps, kc])
                                dv = kb - mc * 4
                                if 0 <= dv < 4:
                                    DV(lambda pt=pt, dv=dv: V.tensor_tensor(
                                        pt[:], pt[:],
                                        cmask[:, dv * CT:(dv + 1) * CT],
                                        ALU.mult), [pt], [pt, cmask])
                                if pend is not None:
                                    do_av(acc, hl, nkc)
                                pend = (kb, pt, vsl, kb0)
                        do_av(acc, hl, nkc)
                        DV(lambda acc=acc: V.reciprocal(
                            Z_ROW, acc[64:65, 0:CT]), [rows], [acc])
                        MM(ps_b[0:64, 0:CT], ones1f[:, 0:64], Z_ROW,
                           True, True)
                        DV(lambda: V.tensor_copy(rb64[:], ps_b[0:64, 0:CT]),
                           [rb64], [ps_b])
                        oc, ob = h // 2, (h % 2) * 64
                        DV(lambda acc=acc, oc=oc, ob=ob: V.tensor_tensor(
                            o_fm[ob:ob + 64, oc * CT:(oc + 1) * CT],
                            acc[0:64, 0:CT], rb64[:], ALU.mult),
                           [o_fm], [acc, rb64])

                    LD(xc[:].rearrange("p (c t) -> p c t", c=8),
                       xT.rearrange("(c p) t -> p c t", p=128)
                       [:, :, mc * CT:(mc + 1) * CT])
                    for dc in range(8):
                        wb = wob[dc % 2]
                        p.op("G", lambda wb=wb, dc=dc: G.dma_start(
                            wb[:].rearrange("p (c n) -> p c n", c=8),
                            wo_r[dc].rearrange("(c p) n -> p c n", p=128)),
                             outs=[wb], ins=[wo_r], inc=16)
                        for ic in range(8):
                            p.op("T", lambda wb=wb, ic=ic: T.matmul(
                                ps_a[:, 0:CT],
                                wb[:, ic * 128:(ic + 1) * 128],
                                o_fm[:, ic * CT:(ic + 1) * CT],
                                start=(ic == 0), stop=(ic == 7)),
                                 outs=[ps_a],
                                 ins=[wb, o_fm]
                                 + ([] if ic == 0 else [ps_a]))
                        DV(lambda dc=dc: V.tensor_tensor(
                            x2[:, dc * CT:(dc + 1) * CT], ps_a[:, 0:CT],
                            xc[:, dc * CT:(dc + 1) * CT], ALU.add),
                           [x2], [ps_a, xc])
                    if DEBUG:
                        LD(dbg_x2.rearrange("m (c p) t -> m p c t", p=128)
                           [mci], x2[:].rearrange("p (c t) -> p c t", c=8))
                    layernorm(x2[:], h_mc)

                    for dcd in range(32):
                        par = dcd % 2
                        wg_b = wgu[par]
                        gc, un = gch[par], uch[par]
                        psg = [ps_a, ps_s0][par]
                        psu = [ps_b, ps_s1][par]
                        p.op("G", lambda wg_b=wg_b, dcd=dcd: G.dma_start(
                            wg_b[:, 0:1024].rearrange("p (c n) -> p c n",
                                                      c=8),
                            wg_d.rearrange("(c p) n -> p c n", p=128)
                            [:, :, dcd * 128:(dcd + 1) * 128]),
                             outs=[wg_b], ins=[wg_d], inc=16)
                        p.op("G", lambda wg_b=wg_b, dcd=dcd: G.dma_start(
                            wg_b[:, 1024:2048].rearrange(
                                "p (c n) -> p c n", c=8),
                            wu_d.rearrange("(c p) n -> p c n", p=128)
                            [:, :, dcd * 128:(dcd + 1) * 128]),
                             outs=[wg_b], ins=[wu_d], inc=16)
                        for base, dst_ps in ((0, psg), (1024, psu)):
                            for dc in range(8):
                                p.op("T", lambda dst_ps=dst_ps, wg_b=wg_b,
                                     base=base, dc=dc: T.matmul(
                                         dst_ps[:, 0:CT],
                                         wg_b[:, base + dc * 128:
                                              base + dc * 128 + 128],
                                         h_mc[:, dc * CT:(dc + 1) * CT],
                                         start=(dc == 0), stop=(dc == 7)),
                                     outs=[dst_ps],
                                     ins=[wg_b, h_mc]
                                     + ([] if dc == 0 else [dst_ps]))
                        p.op("SC", lambda gc=gc, psg=psg, dcd=dcd:
                             SC.activation(gc[:], psg[:, 0:CT], AF.Silu,
                                           bias=bg_sb[:, dcd:dcd + 1]),
                             outs=[gc], ins=[psg, bg_sb])
                        p.op("SC", lambda un=un, psu=psu, dcd=dcd:
                             SC.activation(un[:], psu[:, 0:CT],
                                           AF.Identity,
                                           bias=bu_sb[:, dcd:dcd + 1]),
                             outs=[un], ins=[psu, bu_sb])
                        DV(lambda gc=gc, un=un: V.tensor_mul(
                            gc[:], gc[:], un[:]), [gc], [gc, un])
                        LD(m_dram[dcd * 128:(dcd + 1) * 128, :], gc[:])

                    for qc in range(4):
                        wq_b = wqb[qc % 2]
                        p.op("G", lambda wq_b=wq_b, qc=qc: G.dma_start(
                            wq_b[:].rearrange("p (c n) -> p c n", c=8),
                            wq_r[qc].rearrange("(c p) n -> p c n", p=128)),
                             outs=[wq_b], ins=[wq_r], inc=16)
                        for dc in range(8):
                            p.op("T", lambda wq_b=wq_b, dc=dc: T.matmul(
                                ps_a[:, 0:CT],
                                wq_b[:, dc * 128:(dc + 1) * 128],
                                h_mc[:, dc * CT:(dc + 1) * CT],
                                start=(dc == 0), stop=(dc == 7)),
                                 outs=[ps_a],
                                 ins=[wq_b, h_mc]
                                 + ([] if dc == 0 else [ps_a]))
                        DV(lambda qc=qc: V.tensor_copy(
                            qe_fm[:, qc * CT:(qc + 1) * CT], ps_a[:, 0:CT]),
                           [qe_fm], [ps_a])
                    for tt in range(4):
                        for qc in range(4):
                            MM(ps_g[:, 0:512],
                               qe_fm[:, qc * CT + tt * 128:
                                     qc * CT + tt * 128 + 128],
                               kbd_sb[:, qc * 512:(qc + 1) * 512],
                               qc == 0, qc == 3)
                        sxq = sxq_t[tt]
                        DV(lambda: V.tensor_copy(sim_f[:], ps_g[:, 0:512]),
                           [sim_f], [ps_g])
                        DV(lambda: V.tensor_scalar_mul(sxf[:], sim_f[:],
                                                       QSC), [sxf], [sim_f])
                        DV(lambda sxq=sxq: V.tensor_copy(sxq[:], sxf[:]),
                           [sxq], [sxf])

                    def transp_tt(tt):
                        ga = gacc[tt % 2]
                        for pp in range(8):
                            for ci in range(4):
                                ec = pp * 4 + ci
                                p.op("T", lambda ec=ec, ci=ci, ga=ga:
                                     T.matmul(
                                         ps_tp[:, ci * 128:(ci + 1) * 128],
                                         ga[:, ec * 128:(ec + 1) * 128],
                                         ident[:], start=True, stop=True),
                                     outs=[ps_tp],
                                     ins=[ga, ident]
                                     + ([] if ci == 0 else [ps_tp]))
                            tv = tevac[pp % 2]
                            DV(lambda tv=tv: V.tensor_copy(tv[:],
                                                           ps_tp[:]),
                               [tv], [ps_tp])
                            LD(gate_dram.rearrange("(c p) t -> p c t",
                                                   p=128)
                               [:, pp * 4:(pp + 1) * 4,
                                tt * 128:(tt + 1) * 128],
                               tv[:].rearrange("p (c t) -> p c t", c=4))

                    def a_block(ecs):
                        for ec in ecs:
                            de_b = deb[ec % 2]
                            ws_b = wsil[ec % 2]
                            psw = [ps_a, ps_b][ec % 2]
                            p.op("G", lambda de_b=de_b, ec=ec: G.dma_start(
                                de_b[:].rearrange("p (c n) -> p c n", c=8),
                                deT_d.rearrange("(c p) n -> p c n", p=128)
                                [:, :, ec * 128:(ec + 1) * 128]),
                                 outs=[de_b], ins=[deT_d], inc=16)
                            for dc in range(8):
                                p.op("T", lambda psw=psw, de_b=de_b, dc=dc:
                                     T.matmul(
                                         psw[:, 0:CT],
                                         de_b[:, dc * 128:(dc + 1) * 128],
                                         h_mc[:, dc * CT:(dc + 1) * CT],
                                         start=(dc == 0), stop=(dc == 7)),
                                     outs=[psw],
                                     ins=[de_b, h_mc]
                                     + ([] if dc == 0 else [psw]))
                            p.op("SC", lambda ws_b=ws_b, psw=psw:
                                 SC.activation(ws_b[:], psw[:, 0:CT],
                                               AF.Silu),
                                 outs=[ws_b], ins=[psw])
                            LD(a_dram[ec * 128:(ec + 1) * 128, :], ws_b[:])

                    for tt in range(4):
                        sxq = sxq_t[tt]
                        ga = gacc[tt % 2]
                        DV(lambda sxq=sxq: V.tensor_copy(sxf[:], sxq[:]),
                           [sxf], [sxq])
                        for he in range(HE):
                            DV(lambda he=he: V.max(
                                out=v8[:, he * 8:he * 8 + 8],
                                in_=sxf[:, he * 64:he * 64 + 64]),
                               [v8], [sxf])
                            DV(lambda he=he: V.max(
                                out=vy8[:, he * 8:he * 8 + 8],
                                in_=sxf[:, 256 + he * 64:
                                        256 + he * 64 + 64]),
                               [vy8], [sxf])
                            DV(lambda he=he: V.tensor_tensor(
                                combo[:, he * 64:(he + 1) * 64].rearrange(
                                    "p (a b) -> p a b", b=8),
                                v8[:, he * 8:he * 8 + 8].rearrange(
                                    "p (a o) -> p a o", o=1)
                                .to_broadcast([128, 8, 8]),
                                vy8[:, he * 8:he * 8 + 8].rearrange(
                                    "p (o b) -> p o b", o=1)
                                .to_broadcast([128, 8, 8]),
                                ALU.add), [combo], [v8, vy8])
                        for he in range(HE):
                            DV(lambda he=he: V.max(
                                out=v8[:, he * 8:he * 8 + 8],
                                in_=combo[:, he * 64:(he + 1) * 64]),
                               [v8], [combo])
                        for he in range(HE):
                            hs = slice(he * 8, he * 8 + 8)
                            DV(lambda he=he: V.tensor_copy(
                                tau8[:, he:he + 1],
                                v8[:, he * 8 + 7:he * 8 + 8]),
                               [tau8], [v8])
                            DV(lambda he=he, hs=hs: V.tensor_scalar(
                                e8[:, hs], v8[:, hs],
                                v8[:, he * 8:he * 8 + 1], None,
                                op0=ALU.subtract), [e8], [v8])
                            p.op("SC", lambda hs=hs: SC.activation(
                                e8[:, hs], e8[:, hs], AF.Exp,
                                scale=1.0 / QSC), outs=[e8], ins=[e8])
                            DV(lambda he=he, hs=hs: V.reduce_sum(
                                zsum[:, he:he + 1], e8[:, hs], axis=AXX),
                               [zsum], [e8])
                            DV(lambda he=he: V.tensor_scalar(
                                kt[:, he:he + 1], tau8[:, he:he + 1],
                                v8[:, he * 8:he * 8 + 1], None,
                                op0=ALU.subtract), [kt], [tau8, v8])
                            p.op("SC", lambda he=he: SC.activation(
                                kt[:, he:he + 1], kt[:, he:he + 1],
                                AF.Exp, scale=1.0 / QSC),
                                 outs=[kt], ins=[kt])
                            DV(lambda he=he: V.reciprocal(
                                zsum[:, he:he + 1], zsum[:, he:he + 1]),
                               [zsum], [zsum])
                            DV(lambda he=he: V.tensor_mul(
                                kt[:, he:he + 1], kt[:, he:he + 1],
                                zsum[:, he:he + 1]), [kt], [kt, zsum])
                            DV(lambda he=he: V.tensor_scalar_mul(
                                taub[:, he:he + 1], tau8[:, he:he + 1],
                                -1.0 / QSC), [taub], [tau8])
                            p.op("SC", lambda he=he: SC.activation(
                                exab[:, he * 64:(he + 1) * 64],
                                sxf[:, he * 64:he * 64 + 64], AF.Exp,
                                bias=taub[:, he:he + 1], scale=1.0 / QSC),
                                 outs=[exab], ins=[sxf, taub])
                            p.op("SC", lambda he=he: SC.activation(
                                exbb[:, he * 64:(he + 1) * 64],
                                sxf[:, 256 + he * 64:256 + he * 64 + 64],
                                AF.Exp, scale=1.0 / QSC),
                                 outs=[exbb], ins=[sxf])
                            DV(lambda he=he: V.tensor_tensor(
                                d16[:].rearrange("p (a b) -> p a b", b=64),
                                sxq[:, he * 64:he * 64 + 64].rearrange(
                                    "p (a o) -> p a o", o=1)
                                .to_broadcast([128, 64, 64]),
                                sxq[:, 256 + he * 64:256 + he * 64 + 64]
                                .rearrange("p (o b) -> p o b", o=1)
                                .to_broadcast([128, 64, 64]),
                                ALU.add), [d16], [sxq])
                            DV(lambda he=he: V.tensor_scalar(
                                mkb[:], d16[:], tau8[:, he:he + 1], None,
                                op0=ALU.is_ge), [mkb], [d16, tau8])
                            DV(lambda he=he: V.tensor_tensor(
                                prodb[:].rearrange("p (a b) -> p a b",
                                                   b=64),
                                exab[:, he * 64:(he + 1) * 64].rearrange(
                                    "p (a o) -> p a o", o=1)
                                .to_broadcast([128, 64, 64]),
                                exbb[:, he * 64:(he + 1) * 64].rearrange(
                                    "p (o b) -> p o b", o=1)
                                .to_broadcast([128, 64, 64]),
                                ALU.mult), [prodb], [exab, exbb])
                            dst = ga if he == 0 else prodb
                            DV(lambda dst=dst, he=he:
                               V.scalar_tensor_tensor(
                                   dst[:], prodb[:], kt[:, he:he + 1],
                                   mkb[:], op0=ALU.mult, op1=ALU.mult),
                               [dst], [prodb, kt, mkb])
                            if he > 0:
                                DV(lambda ga=ga: V.tensor_add(
                                    ga[:], ga[:], prodb[:]),
                                   [ga], [ga, prodb])
                        if tt > 0:
                            transp_tt(tt - 1)
                        a_block(range(tt * 8, tt * 8 + 8))
                    transp_tt(3)

                    banks = [ps_a, ps_b, ps_s0, ps_s1, ps_c0, ps_c1,
                             ps_g, ps_tp]
                    for dcd in range(32):
                        par = dcd % 2
                        wd_b = wdb[par]
                        ml_b = mld[par]
                        p.op("G", lambda wd_b=wd_b, dcd=dcd:
                             G.dma_start(
                                 wd_b[:],
                                 wd_d[dcd * 128:(dcd + 1) * 128, :]),
                             outs=[wd_b], ins=[wd_d], inc=16)
                        LD(ml_b[:],
                           m_dram[dcd * 128:(dcd + 1) * 128, :])
                        for dc in range(8):
                            p.op("T", lambda dc=dc, wd_b=wd_b,
                                 ml_b=ml_b, dcd=dcd: T.matmul(
                                     banks[dc][:, 0:CT],
                                     wd_b[:, dc * 128:(dc + 1) * 128],
                                     ml_b[:], start=(dcd == 0),
                                     stop=False),
                                 outs=[banks[dc]],
                                 ins=[wd_b, ml_b]
                                 + ([] if dcd == 0 else [banks[dc]]))
                    for ec in range(32):
                        par = ec % 2
                        ue_b = ueb[par]
                        ml_b = mld[par]
                        gl_b = gld[par]
                        p.op("G", lambda ue_b=ue_b, ec=ec:
                             G.dma_start(
                                 ue_b[:],
                                 ue_d[ec * 128:(ec + 1) * 128, :]),
                             outs=[ue_b], ins=[ue_d], inc=16)
                        LD(ml_b[:],
                           a_dram[ec * 128:(ec + 1) * 128, :])
                        LD(gl_b[:],
                           gate_dram[ec * 128:(ec + 1) * 128, :])
                        DV(lambda ml_b=ml_b, gl_b=gl_b: V.tensor_mul(
                            ml_b[:], ml_b[:], gl_b[:]),
                           [ml_b], [ml_b, gl_b])
                        for dc in range(8):
                            p.op("T", lambda dc=dc, ue_b=ue_b,
                                 ml_b=ml_b, ec=ec: T.matmul(
                                     banks[dc][:, 0:CT],
                                     ue_b[:, dc * 128:(dc + 1) * 128],
                                     ml_b[:], start=False,
                                     stop=(ec == 31)),
                                 outs=[banks[dc]],
                                 ins=[ue_b, ml_b, banks[dc]])
                    for dc in range(8):
                        DV(lambda dc=dc: V.scalar_tensor_tensor(
                            xc[:, dc * CT:(dc + 1) * CT],
                            banks[dc][:, 0:CT],
                            bd_sb[:, dc:dc + 1],
                            x2[:, dc * CT:(dc + 1) * CT],
                            op0=ALU.add, op1=ALU.add),
                           [xc], [banks[dc], bd_sb, x2])
                    LD(out_d.rearrange("m (c p) t -> m p c t", p=128)[mci],
                       xc[:].rearrange("p (c t) -> p c t", c=8))

            p.finish("SY")

    return nc


_NC_CACHE = None
_LAST_RESULTS = None


def kernel(L0_attn_x, L0_attn_Wqkv, L0_attn_Wo, L0_attn_log_xi,
           L0_attn_pi_gate_logit, L0_attn_e_gate_logit, L0_ffn_Wg, L0_ffn_bg,
           L0_ffn_Wu, L0_ffn_bu, L0_ffn_Wd, L0_ffn_bd, L0_ffn_Wq, L0_ffn_keys,
           L0_ffn_down_embed, L0_ffn_up_embed):
    global _NC_CACHE, _LAST_RESULTS

    x = np.asarray(L0_attn_x, np.float32)
    Wqkv = np.asarray(L0_attn_Wqkv, np.float32)

    perm = np.empty(3 * D, np.int64)
    for g in range(4):
        for half in range(2):
            for hh in range(4):
                for f in range(32):
                    c = g * 256 + half * 128 + hh * 32 + f
                    row = (4 * g + hh) * 64 + 2 * f + half
                    perm[c] = row
                    perm[D + c] = D + row
    perm[2 * D:] = np.arange(2 * D, 3 * D)
    wqkvT = np.ascontiguousarray(Wqkv[perm].T)

    woT = np.asarray(L0_attn_Wo, np.float32).T
    wo_r = np.ascontiguousarray(woT.reshape(D, 8, 128).transpose(1, 0, 2))
    gates3 = np.ascontiguousarray(np.stack(
        [np.asarray(L0_attn_log_xi, np.float32),
         np.asarray(L0_attn_pi_gate_logit, np.float32),
         np.asarray(L0_attn_e_gate_logit, np.float32)], axis=1))

    wg = np.ascontiguousarray(np.asarray(L0_ffn_Wg, np.float32))
    wu = np.ascontiguousarray(np.asarray(L0_ffn_Wu, np.float32))
    wd = np.ascontiguousarray(np.asarray(L0_ffn_Wd, np.float32))
    bg2 = np.ascontiguousarray(
        np.asarray(L0_ffn_bg, np.float32).reshape(32, 128).T)
    bu2 = np.ascontiguousarray(
        np.asarray(L0_ffn_bu, np.float32).reshape(32, 128).T)
    bd2 = np.ascontiguousarray(
        np.asarray(L0_ffn_bd, np.float32).reshape(8, 128).T)
    wq = np.asarray(L0_ffn_Wq, np.float32)
    wq_r = np.ascontiguousarray(wq.reshape(D, 4, 128).transpose(1, 0, 2))
    keys = np.asarray(L0_ffn_keys, np.float32)
    kbd = np.zeros((512, 512), np.float32)
    for pp in range(2):
        for he in range(HE):
            r0 = pp * 256 + he * 64
            kbd[r0:r0 + 64, r0:r0 + 64] = keys[he, :, pp, :].T
    deT = np.ascontiguousarray(np.asarray(L0_ffn_down_embed, np.float32).T)
    ue = np.ascontiguousarray(np.asarray(L0_ffn_up_embed, np.float32))

    import jax
    import jax.numpy as jnp
    with jax.default_device(jax.local_devices(backend="cpu")[0]):
        kfrac_j = jnp.arange(0, 64, 2, dtype=jnp.float32) / 64
        pi_inv_j = jnp.float32(math.pi) ** (1.0 - kfrac_j)
        pos_j = jnp.arange(S, dtype=jnp.float32)
        th_np = np.asarray(pos_j[:, None] * pi_inv_j[None, :])
    th0 = np.ascontiguousarray(np.tile(th_np.T, (4, 1)).astype(np.float32))

    kpos = np.ascontiguousarray(
        (np.arange(128, dtype=np.float32)[:, None]
         + 128.0 * np.arange(64, dtype=np.float32)[None, :]))

    rselm = np.zeros((16, 4 * 128), np.float32)
    for g in range(4):
        for r in range(128):
            rselm[4 * g + r // 32, g * 128 + r] = 1.0

    cmask = np.zeros((128, 4 * CT), ml_dtypes.bfloat16)
    for dv in range(4):
        for pp in range(128):
            cmask[pp, dv * CT + dv * 128 + pp:(dv + 1) * CT] = 1.0

    if _NC_CACHE is None:
        _NC_CACHE = build_nc()
    nc = _NC_CACHE

    shared = dict(
        wqkvT=wqkvT, wo_r=wo_r, gates3=gates3, wg_d=wg, wu_d=wu, wd_d=wd,
        bg_d=bg2, bu_d=bu2, bd_d=bd2, wq_r=wq_r, kbd_d=kbd, deT_d=deT,
        ue_d=ue, th0_d=th0, kpos_d=kpos, rsel_d=rselm, cmask_d=cmask,
        ident_d=np.eye(128, dtype=ml_dtypes.bfloat16),
    )
    in_maps = []
    for core in range(8):
        b, bc = core // 4, core % 4
        qp = np.concatenate(
            [np.arange(mcc * CT, (mcc + 1) * CT, dtype=np.float32)
             for mcc in CHUNKS[bc]])
        in_maps.append(dict(
            shared,
            xT=np.ascontiguousarray(x[b].T),
            qpos_d=np.ascontiguousarray(np.tile(qp[None, :], (16, 1))),
        ))

    res = run_bass_kernel_spmd(nc, in_maps, core_ids=list(range(8)))
    _LAST_RESULTS = res

    out = np.empty((B, S, D), np.float32)
    for core in range(8):
        b, bc = core // 4, core % 4
        o = res.results[core]["out"]
        for mci, mcc in enumerate(CHUNKS[bc]):
            out[b, mcc * CT:(mcc + 1) * CT, :] = o[mci].T
    return out


# version 18
# speedup vs baseline: 2.5250x; 1.0179x over previous
import math
from contextlib import ExitStack

import numpy as np
import ml_dtypes

import concourse.bass as bass
import concourse.mybir as mybir
from concourse.bass_utils import run_bass_kernel_spmd

F32 = mybir.dt.float32
BF16 = mybir.dt.bfloat16
I16 = mybir.dt.int16
I32 = mybir.dt.int32
AF = mybir.ActivationFunctionType
ALU = mybir.AluOpType
AXX = mybir.AxisListType.X

B, S, D = 2, 8192, 1024
H = 16
NC, CT = 16, 512
HE = 4
QSC = 4096.0
INV8 = 0.125
SHIFT = 8.0
TWO_PI = 2.0 * math.pi
INV2PI = float(np.float32(1.0 / (2.0 * math.pi)))
_bb = np.float32(INV2PI).view(np.int32) & np.int32(-4096)
C_HI = float(np.int32(_bb).view(np.float32))
C_LO = float(np.float32(1.0 / (2.0 * math.pi) - C_HI))

CHUNKS = [sorted([bc, 7 - bc, 8 + bc, 15 - bc]) for bc in range(4)]
DEBUG = False
SERIAL = False


def _bufname(x):
    if isinstance(x, str):
        return x
    n = getattr(x, "name", None)
    if n is not None:
        return n
    return str(x)


class Pipe:

    def __init__(self, engs, sems):
        self.engs = engs
        self.sems = sems
        self.cum = {e: 0 for e in engs}
        self.last_w = {}
        self.last_r = {}
        self.synced = {}

    def op(self, ename, fn, outs, ins, inc=1):
        eng = self.engs[ename]
        targets = {}

        def need(e, v):
            if e == "T" and ename == "T":
                return
            if v > targets.get(e, 0):
                targets[e] = v

        if SERIAL:
            for e in self.engs:
                if self.cum[e]:
                    targets[e] = self.cum[e]
        inames = [_bufname(b) for b in ins]
        onames = [_bufname(b) for b in outs]
        for b in inames:
            w = self.last_w.get(b)
            if w:
                need(*w)
        for b in onames:
            w = self.last_w.get(b)
            if w:
                need(*w)
            for e, v in self.last_r.get(b, {}).items():
                need(e, v)
        for t in list(targets):
            if t in ("SY", "G"):
                targets[t] = self.cum[t]
        for t, val in sorted(targets.items()):
            if self.synced.get((ename, t), 0) < val:
                eng.wait_ge(self.sems[t], val)
                self.synced[(ename, t)] = val
        inst = fn()
        inst.then_inc(self.sems[ename], inc)
        self.cum[ename] += inc
        cv = self.cum[ename]
        for b in inames:
            self.last_r.setdefault(b, {})[ename] = cv
        for b in onames:
            self.last_w[b] = (ename, cv)
            self.last_r[b] = {}
        return inst

    def barrier(self):
        for e in self.engs:
            eng = self.engs[e]
            for t in self.engs:
                if self.cum[t] and \
                        self.synced.get((e, t), 0) < self.cum[t]:
                    eng.wait_ge(self.sems[t], self.cum[t])
                    self.synced[(e, t)] = self.cum[t]

    def finish(self, ename):
        eng = self.engs[ename]
        for t in self.engs:
            if self.cum[t] and self.synced.get((ename, t), 0) < self.cum[t]:
                eng.wait_ge(self.sems[t], self.cum[t])
                self.synced[(ename, t)] = self.cum[t]


def build_nc():
    nc = bass.Bass("TRN2", target_bir_lowering=False)

    ap = lambda t: t.ap()
    xT = ap(nc.dram_tensor("xT", [D, S], F32, kind="ExternalInput"))
    wqkvT = ap(nc.dram_tensor("wqkvT", [D, 3 * D], F32, kind="ExternalInput"))
    wo_r = ap(nc.dram_tensor("wo_r", [8, D, 128], F32, kind="ExternalInput"))
    gates3 = ap(nc.dram_tensor("gates3", [16, 3], F32, kind="ExternalInput"))
    wg_d = ap(nc.dram_tensor("wg_d", [D, 4096], F32, kind="ExternalInput"))
    wu_d = ap(nc.dram_tensor("wu_d", [D, 4096], F32, kind="ExternalInput"))
    wd_d = ap(nc.dram_tensor("wd_d", [4096, D], F32, kind="ExternalInput"))
    bg_d = ap(nc.dram_tensor("bg_d", [128, 32], F32, kind="ExternalInput"))
    bu_d = ap(nc.dram_tensor("bu_d", [128, 32], F32, kind="ExternalInput"))
    bd_d = ap(nc.dram_tensor("bd_d", [128, 8], F32, kind="ExternalInput"))
    wq_r = ap(nc.dram_tensor("wq_r", [4, D, 128], F32, kind="ExternalInput"))
    kbd_d = ap(nc.dram_tensor("kbd_d", [512, 512], F32, kind="ExternalInput"))
    deT_d = ap(nc.dram_tensor("deT_d", [D, 4096], F32, kind="ExternalInput"))
    ue_d = ap(nc.dram_tensor("ue_d", [4096, D], F32, kind="ExternalInput"))
    th0_d = ap(nc.dram_tensor("th0_d", [128, S], F32, kind="ExternalInput"))
    qpos_d = ap(nc.dram_tensor("qpos_d", [16, 4 * CT], F32, kind="ExternalInput"))
    kpos_d = ap(nc.dram_tensor("kpos_d", [128, 64], F32, kind="ExternalInput"))
    rsel_d = ap(nc.dram_tensor("rsel_d", [16, 4 * 128], F32, kind="ExternalInput"))
    cmask_d = ap(nc.dram_tensor("cmask_d", [128, 4 * CT], BF16, kind="ExternalInput"))
    ident_d = ap(nc.dram_tensor("ident_d", [128, 128], BF16, kind="ExternalInput"))

    out_d = ap(nc.dram_tensor("out", [4, D, CT], F32, kind="ExternalOutput"))

    kind = "ExternalOutput" if DEBUG else "Internal"
    k_dram = ap(nc.dram_tensor("k_dram", [D, S], BF16, kind=kind))
    v_dram = ap(nc.dram_tensor("v_dram", [4, 64, 128, 4 * 65], BF16, kind=kind))
    q_dram = ap(nc.dram_tensor("q_dram", [D, 4 * CT], BF16, kind=kind))
    g_dram = ap(nc.dram_tensor("g_dram", [3, 16, 4 * CT], BF16, kind=kind))
    srow_dram = ap(nc.dram_tensor("srow_dram", [2, 16], F32))
    gate_dram = ap(nc.dram_tensor("gate_dram", [4096, CT], BF16, kind=kind))
    a_dram = ap(nc.dram_tensor("a_dram", [4096, CT], BF16, kind=kind))
    m_dram = ap(nc.dram_tensor("m_dram", [4096, CT], BF16, kind=kind))
    dbg_x2 = ap(nc.dram_tensor("dbg_x2", [4, D, CT], F32, kind=kind)) \
        if DEBUG else None

    with ExitStack() as st0:
        def mk(stack, sfx=""):
            def sb(name, shape, dtype):
                return stack.enter_context(
                    nc.sbuf_tensor(name + sfx, shape, dtype))
            return sb

        sb0 = mk(st0)
        ones1f = sb0("ones1f", [1, 128], F32)
        onesb = sb0("onesb", [128, 1], BF16)
        eps_t = sb0("eps_t", [1, 1], F32)
        quart_t = sb0("quart_t", [128, 1], F32)
        invD_t = sb0("invD_t", [1, 1], F32)
        neg1_t = sb0("neg1_t", [1, 1], F32)
        magic_t = sb0("magic_t", [128, 1], F32)
        nmagic_t = sb0("nmagic_t", [128, 1], F32)
        ident = sb0("ident", [128, 128], BF16)
        gsig = sb0("gsig", [16, 3], F32)
        xi_r = sb0("xi_r", [16, 1], F32)
        pig = sb0("pig", [16, 1], F32)
        egf = sb0("egf", [16, 1], F32)
        chh = sb0("chh", [16, 1], F32)
        cm8 = sb0("cm8", [16, 1], F32)
        tmp16 = sb0("tmp16", [16, 4], F32)
        pigcol = sb0("pigcol", [128, 4], F32)
        crep = sb0("crep", [128, 16], F32)
        kc = sb0("kc", [128, 64 * 16], F32)
        cmask = sb0("cmask", [128, 4 * CT], BF16)
        rows = sb0("rows", [1, 6 * CT], F32)
        rb_sb = sb0("rb_sb", [128, CT], F32)
        mrb_sb = sb0("mrb_sb", [128, CT], F32)
        xb1 = [sb0("xb1" + str(i), [128, CT], BF16) for i in range(2)]
        sq1 = [sb0("sq1" + str(i), [128, CT], BF16) for i in range(2)]
        xc = sb0("xc", [128, 8 * CT], F32)

        ps_a = st0.enter_context(nc.psum_tensor("ps_a", [128, 512], F32))
        ps_b = st0.enter_context(nc.psum_tensor("ps_b", [128, 512], F32))
        ps_s0 = st0.enter_context(nc.psum_tensor("ps_s0", [128, 512], F32))
        ps_s1 = st0.enter_context(nc.psum_tensor("ps_s1", [128, 512], F32))
        ps_c0 = st0.enter_context(nc.psum_tensor("ps_c0", [128, 512], F32))
        ps_c1 = st0.enter_context(nc.psum_tensor("ps_c1", [128, 512], F32))
        ps_tp = st0.enter_context(nc.psum_tensor("ps_tp", [128, 512], F32))
        ps_g = st0.enter_context(nc.psum_tensor("ps_g", [128, 512], F32))

        sems = {n: st0.enter_context(nc.semaphore("sem_" + n))
                for n in ("V", "SC", "T", "G", "SY")}
        blk = st0.enter_context(nc.Block())

        V, SC, T, G, SY = nc.vector, nc.scalar, nc.tensor, nc.gpsimd, nc.sync
        ENGS = {"V": V, "SC": SC, "T": T, "G": G, "SY": SY}

        M_ROW = rows[:, 0:CT]
        V_ROW = rows[:, CT:2 * CT]
        R_ROW = rows[:, 2 * CT:3 * CT]
        MR_ROW = rows[:, 3 * CT:4 * CT]
        Z_ROW = rows[:, 4 * CT:5 * CT]

        engs_l = [V, SC, T, G, SY]
        idxs = [e.compute_val(e.partition_id() % 4) for e in engs_l]

        for bc in nc.Switch(engines=engs_l, index=idxs, n=4):
            p = Pipe(ENGS, sems)
            ch_list = CHUNKS[bc]

            def LD(dst, src, eng="SY", **kw):
                p.op(eng, lambda: ENGS[eng].dma_start(dst, src, **kw),
                     outs=[dst], ins=[src], inc=16)

            def MM(out, lhsT, rhs, start, stop):
                ins = [lhsT, rhs] + ([] if start else [out])
                p.op("T", lambda: T.matmul(out, lhsT, rhs, start=start,
                                           stop=stop),
                     outs=[out], ins=ins)

            def DV(fn, outs, ins):
                p.op("V", fn, outs=outs, ins=ins)

            def ACT(out, in_, func, bias=0.0, scale=1.0):
                ins = [in_] + ([bias] if isinstance(bias, bass.AP) else [])
                p.op("SC", lambda: SC.activation(out, in_, func, bias=bias,
                                                 scale=scale),
                     outs=[out], ins=ins)

            DV(lambda: V.memset(ones1f[:], 1.0), [ones1f], [])
            DV(lambda: V.memset(onesb[:], 1.0), [onesb], [])
            DV(lambda: V.memset(eps_t[:], 1e-5), [eps_t], [])
            DV(lambda: V.memset(quart_t[:], 0.25), [quart_t], [])
            DV(lambda: V.memset(invD_t[:], 1.0 / D), [invD_t], [])
            DV(lambda: V.memset(neg1_t[:], -1.0), [neg1_t], [])
            DV(lambda: V.memset(magic_t[:], 12582912.0), [magic_t], [])
            DV(lambda: V.memset(nmagic_t[:], -12582912.0), [nmagic_t], [])
            LD(ident[:], ident_d)
            LD(gsig[:], gates3)
            LD(rb_sb[:, 0:64], kpos_d)
            LD(cmask[:], cmask_d)
            ACT(xi_r[:], gsig[:, 0:1], AF.Exp)
            DV(lambda: V.reciprocal(xi_r[:], xi_r[:]), [xi_r], [xi_r])
            for col, dst in ((1, pig), (2, egf)):
                ACT(dst[:], gsig[:, col:col + 1], AF.Exp, scale=-1.0)
                DV(lambda d=dst: V.tensor_scalar_add(d[:], d[:], 1.0),
                   [dst], [dst])
                DV(lambda d=dst: V.reciprocal(d[:], d[:]), [dst], [dst])
            DV(lambda: V.tensor_mul(chh[:], egf[:], xi_r[:]),
               [chh], [egf, xi_r])
            DV(lambda: V.tensor_scalar_mul(cm8[:], chh[:], -8.0),
               [cm8], [chh])
            DV(lambda: V.tensor_copy(tmp16[:, 0:1], cm8[:]), [tmp16], [cm8])
            DV(lambda: V.tensor_copy(tmp16[:, 1:2], chh[:]), [tmp16], [chh])
            LD(srow_dram.rearrange("a p -> p a"), tmp16[:, 0:2],
               allow_slow_non_contiguous=True)
            LD(crep[0:1, 0:16], srow_dram[1:2, :])
            MM(ps_a[:, 0:16], ones1f[:], crep[0:1, 0:16], True, True)
            DV(lambda: V.tensor_copy(crep[:], ps_a[:, 0:16]), [crep], [ps_a])
            DV(lambda: V.tensor_tensor(
                kc[:].rearrange("p (k h) -> p k h", h=16),
                rb_sb[:, 0:64].rearrange("p (k o) -> p k o", o=1)
                .to_broadcast([128, 64, 16]),
                crep[:].rearrange("p (o h) -> p o h", o=1)
                .to_broadcast([128, 64, 16]),
                ALU.mult), [kc], [rb_sb, crep])
            DV(lambda: V.tensor_scalar_add(kc[:], kc[:], -SHIFT), [kc], [kc])
            DV(lambda: V.tensor_copy(tmp16[:, 2:3], pig[:]), [tmp16], [pig])
            with ExitStack() as st1:
                sb1 = mk(st1, f"_{bc}")
                rsel = sb1("rsel", [16, 4 * 128], F32)
                qpos = sb1("qpos", [16, 4 * CT], F32)
                g_pm = sb1("g_pm", [16, 3 * 4 * CT], BF16)
                gt1 = sb1("gt1", [16, 4 * CT], F32)
                gt2 = sb1("gt2", [16, 4 * CT], F32)
                LD(rsel[:], rsel_d)
                LD(qpos[:], qpos_d)
                for g in range(4):
                    MM(ps_a[:, g:g + 1], rsel[:, g * 128:(g + 1) * 128],
                       tmp16[:, 2:3], True, True)
                DV(lambda: V.tensor_copy(pigcol[:], ps_a[:, 0:4]),
                   [pigcol], [ps_a])
                DV(lambda: V.tensor_scalar(gt1[:], qpos[:], cm8[:, 0:1],
                                           None, op0=ALU.mult),
                   [gt1], [qpos, cm8])
                DV(lambda: V.tensor_copy(g_pm[:, 0:2048], gt1[:]),
                   [g_pm], [gt1])
                DV(lambda: V.tensor_sub(gt2[:], gt1[:], g_pm[:, 0:2048]),
                   [gt2], [gt1, g_pm])
                DV(lambda: V.tensor_copy(g_pm[:, 2048:4096], gt2[:]),
                   [g_pm], [gt2])
                DV(lambda: V.tensor_sub(gt1[:], gt2[:], g_pm[:, 2048:4096]),
                   [gt1], [gt2, g_pm])
                DV(lambda: V.tensor_copy(g_pm[:, 4096:6144], gt1[:]),
                   [g_pm], [gt1])
                LD(g_dram.rearrange("r h t -> h r t"),
                   g_pm[:].rearrange("h (r t) -> h r t", r=3))

            def layernorm(src, out_bf):
                for dc in range(8):
                    sl = slice(dc * CT, (dc + 1) * CT)
                    xb = xb1[dc % 2]
                    sq = sq1[dc % 2]
                    ACT(xb[:], src[:, sl], AF.Identity)
                    DV(lambda xb=xb, sq=sq: V.tensor_mul(sq[:], xb[:],
                                                         xb[:]),
                       [sq], [xb])
                    MM(ps_s0[0:1, 0:CT], onesb[:], xb[:], dc == 0, dc == 7)
                    MM(ps_s1[0:1, 0:CT], onesb[:], sq[:], dc == 0, dc == 7)
                DV(lambda: V.tensor_scalar_mul(M_ROW, ps_s0[0:1, 0:CT],
                                               1.0 / D), [rows], [ps_s0])
                DV(lambda: V.tensor_mul(Z_ROW, M_ROW, M_ROW), [rows], [rows])
                DV(lambda: V.scalar_tensor_tensor(
                    V_ROW, ps_s1[0:1, 0:CT], invD_t[0:1, 0:1], Z_ROW,
                    op0=ALU.mult, op1=ALU.subtract),
                   [rows], [ps_s1, invD_t, rows])
                ACT(V_ROW, V_ROW, AF.Ln, bias=eps_t[0:1, 0:1])
                ACT(R_ROW, V_ROW, AF.Exp, scale=-0.5)
                DV(lambda: V.scalar_tensor_tensor(
                    MR_ROW, M_ROW, neg1_t[0:1, 0:1], R_ROW,
                    op0=ALU.mult, op1=ALU.mult),
                   [rows], [rows, neg1_t])
                MM(ps_a[:, 0:CT], ones1f[:], R_ROW, True, True)
                MM(ps_b[:, 0:CT], ones1f[:], MR_ROW, True, True)
                ACT(rb_sb[:], ps_a[:, 0:CT], AF.Identity)
                ACT(mrb_sb[:], ps_b[:, 0:CT], AF.Identity)
                for dc in range(8):
                    sl = slice(dc * CT, (dc + 1) * CT)
                    DV(lambda sl=sl: V.tensor_tensor(
                        out_bf[:, sl], src[:, sl], rb_sb[:], ALU.mult),
                       [out_bf], [src, rb_sb])
                    DV(lambda sl=sl: V.tensor_tensor(
                        out_bf[:, sl], out_bf[:, sl], mrb_sb[:], ALU.add),
                       [out_bf], [out_bf, mrb_sb])

            p.barrier()

            with ExitStack() as st2:
                sb2 = mk(st2, f"_{bc}")
                wqkv_sb = sb2("wqkv_sb", [128, 8 * 3072], BF16)
                lnb = sb2("lnb", [128, 8 * CT], BF16)
                xcb = sb2("xcb", [128, 8 * CT], F32)
                thf = sb2("thf", [128, 4 * CT], F32)
                frac = sb2("frac", [128, 4 * CT], F32)
                sc2 = sb2("sc2", [128, 4 * CT], F32)
                sc3 = sb2("sc3", [128, 4 * CT], F32)
                csnb = sb2("csnb", [128, 4 * CT], BF16)
                ccsb = sb2("ccsb", [128, 4 * CT], BF16)
                kqA = sb2("kqA", [128, CT], BF16)
                kqB = sb2("kqB", [128, CT], BF16)
                rotA = [sb2(f"rotA{i}", [128, CT], BF16) for i in range(2)]
                rotB = [sb2(f"rotB{i}", [128, CT], BF16) for i in range(2)]
                rtmp = sb2("rtmp", [128, CT], BF16)
                v_loc = [sb2(f"v_loc{i}", [128, 4 * 16 * 65], BF16)
                         for i in range(2)]

                LD(wqkv_sb[:].rearrange("p (c n) -> p c n", c=8),
                   wqkvT.rearrange("(c p) n -> p c n", p=128), "G")

                for tc in range(NC):
                    mine = tc in ch_list
                    mci = ch_list.index(tc) if mine else -1

                    LD(sc3[:, 0:CT], th0_d[:, tc * CT:(tc + 1) * CT])
                    DV(lambda: V.tensor_tensor(
                        thf[:].rearrange("p (g t) -> p g t", g=4),
                        sc3[:, 0:CT].rearrange("p (o t) -> p o t", o=1)
                        .to_broadcast([128, 4, CT]),
                        pigcol[:].rearrange("p (g o) -> p g o", o=1)
                        .to_broadcast([128, 4, CT]),
                        ALU.mult), [thf], [sc3, pigcol])
                    DV(lambda: V.tensor_single_scalar(
                        sc2[:].bitcast(I32), thf[:].bitcast(I32), -4096,
                        ALU.bitwise_and), [sc2], [thf])
                    DV(lambda: V.tensor_sub(sc3[:], thf[:], sc2[:]),
                       [sc3], [thf, sc2])
                    ACT(thf[:], sc2[:], AF.Identity, scale=C_HI)
                    ACT(frac[:], thf[:], AF.Identity, bias=magic_t[:, 0:1])
                    ACT(frac[:], frac[:], AF.Identity,
                        bias=nmagic_t[:, 0:1])
                    DV(lambda: V.tensor_sub(thf[:], thf[:], frac[:]),
                       [thf], [thf, frac])
                    ACT(sc2[:], sc2[:], AF.Identity, scale=C_LO)
                    ACT(sc3[:], sc3[:], AF.Identity, scale=INV2PI)
                    DV(lambda: V.tensor_add(thf[:], thf[:], sc2[:]),
                       [thf], [thf, sc2])
                    DV(lambda: V.tensor_add(thf[:], thf[:], sc3[:]),
                       [thf], [thf, sc3])
                    ACT(frac[:], thf[:], AF.Identity, bias=magic_t[:, 0:1])
                    ACT(frac[:], frac[:], AF.Identity,
                        bias=nmagic_t[:, 0:1])
                    DV(lambda: V.tensor_sub(frac[:], thf[:], frac[:]),
                       [frac], [thf, frac])
                    ACT(csnb[:], frac[:], AF.Sin, scale=TWO_PI)
                    ACT(frac[:], frac[:], AF.Identity,
                        bias=quart_t[:, 0:1])
                    DV(lambda: V.tensor_scalar(thf[:], frac[:], 0.5, None,
                                               op0=ALU.is_gt), [thf], [frac])
                    DV(lambda: V.tensor_sub(frac[:], frac[:], thf[:]),
                       [frac], [frac, thf])
                    ACT(ccsb[:], frac[:], AF.Sin, scale=TWO_PI)

                    src = xc if tc % 2 == 0 else xcb
                    LD(src[:].rearrange("p (c t) -> p c t", c=8),
                       xT.rearrange("(c p) t -> p c t", p=128)
                       [:, :, tc * CT:(tc + 1) * CT])
                    layernorm(src[:], lnb)

                    def proj_rot(col_base, dst_dram, dst_col):
                        for g in range(4):
                            rA = rotA[g % 2]
                            rB = rotB[g % 2]
                            for half, dst_ps in ((0, ps_a), (1, ps_b)):
                                cb = col_base + g * 256 + half * 128
                                for dc in range(8):
                                    MM(dst_ps[:, 0:CT],
                                       wqkv_sb[:, dc * 3072 + cb:
                                               dc * 3072 + cb + 128],
                                       lnb[:, dc * CT:(dc + 1) * CT],
                                       dc == 0, dc == 7)
                            ACT(kqA[:], ps_a[:, 0:CT], AF.Identity)
                            ACT(kqB[:], ps_b[:, 0:CT], AF.Identity)
                            gs = slice(g * CT, (g + 1) * CT)
                            DV(lambda gs=gs, rA=rA: V.tensor_tensor(
                                rA[:], kqA[:], ccsb[:, gs], ALU.mult),
                               [rA], [kqA, ccsb])
                            DV(lambda gs=gs: V.tensor_tensor(
                                rtmp[:], kqB[:], csnb[:, gs], ALU.mult),
                               [rtmp], [kqB, csnb])
                            DV(lambda rA=rA: V.tensor_sub(rA[:], rA[:],
                                                          rtmp[:]),
                               [rA], [rA, rtmp])
                            DV(lambda gs=gs, rB=rB: V.tensor_tensor(
                                rB[:], kqA[:], csnb[:, gs], ALU.mult),
                               [rB], [kqA, csnb])
                            DV(lambda gs=gs: V.tensor_tensor(
                                rtmp[:], kqB[:], ccsb[:, gs], ALU.mult),
                               [rtmp], [kqB, ccsb])
                            DV(lambda rB=rB: V.tensor_add(rB[:], rB[:],
                                                          rtmp[:]),
                               [rB], [rB, rtmp])
                            for half, rsrc in ((0, rA), (1, rB)):
                                LD(dst_dram.rearrange(
                                    "(hh r) t -> hh r t", r=64)
                                   [g * 4:(g + 1) * 4,
                                    half * 32:half * 32 + 32,
                                    dst_col:dst_col + CT],
                                   rsrc[:])

                    proj_rot(1024, k_dram, tc * CT)
                    if mine:
                        proj_rot(0, q_dram, mci * CT)

                    vl = v_loc[tc % 2]
                    for kcq in range(4):
                        for half, dst_ps in ((0, ps_a), (1, ps_b)):
                            for dc in range(8):
                                MM(dst_ps[:, 0:512],
                                   lnb[:, dc * CT + kcq * 128:
                                       dc * CT + kcq * 128 + 128],
                                   wqkv_sb[:, dc * 3072 + 2048 + half * 512:
                                           dc * 3072 + 2048 + half * 512
                                           + 512],
                                   dc == 0, dc == 7)
                        for half, src_ps in ((0, ps_a), (1, ps_b)):
                            DV(lambda kcq=kcq, half=half, src_ps=src_ps,
                               vl=vl: V.tensor_copy(
                                   bass.AP(vl,
                                           kcq * 1040 + half * 520,
                                           [[4160, 128], [65, 8], [1, 64]]),
                                   src_ps[:, 0:512].rearrange(
                                       "p (h d) -> p h d", h=8)),
                               [vl], [src_ps])
                    DV(lambda vl=vl: V.memset(
                        bass.AP(vl, 64,
                                [[4160, 128], [65, 64], [1, 1]]), 1.0),
                       [vl], [])
                    for g in range(4):
                        LD(v_dram[g].rearrange("k p c -> p k c")
                           [:, tc * 4:(tc + 1) * 4, :],
                           vl[:].rearrange("p (k g c) -> p k g c",
                                           k=4, g=4)[:, :, g, :])

            p.barrier()

            with ExitStack() as st3:
                sb3 = mk(st3, f"_{bc}")
                def dbl(name, shape, dtype):
                    return [sb3(name + str(i), shape, dtype)
                            for i in range(2)]
                wob = dbl("wob", [128, 8 * 128], BF16)
                wqb = dbl("wqb", [128, 8 * 128], BF16)
                kbd_sb = sb3("kbd_sb", [128, 4 * 512], BF16)
                bg_sb = sb3("bg_sb", [128, 32], F32)
                bu_sb = sb3("bu_sb", [128, 32], F32)
                bd_sb = sb3("bd_sb", [128, 8], F32)
                q_sub = dbl("q_sub", [67, CT], BF16)
                k_sub = dbl("k_sub", [67, 1024], BF16)
                v_sub = dbl("v_sub", [128, 8 * 260], BF16)
                p_t = [sb3("p_t" + str(i), [128, CT], BF16)
                       for i in range(3)]
                rb64 = sb3("rb64", [64, CT], F32)
                o_fm = sb3("o_fm", [128, 8 * CT], BF16)
                x2 = sb3("x2", [128, 8 * CT], F32)
                h_mc = sb3("h_mc", [128, 8 * CT], BF16)
                wgu = dbl("wgu", [128, 2 * 1024], BF16)
                wdb = dbl("wdb", [128, 1024], BF16)
                gch = dbl("gch", [128, CT], BF16)
                uch = dbl("uch", [128, CT], BF16)
                qe_fm = sb3("qe_fm", [128, 4 * CT], BF16)
                sim_f = sb3("sim_f", [128, 512], F32)
                sxq_t = [sb3(f"sxq{i}", [128, 512], I16) for i in range(4)]
                sxf = sb3("sxf", [128, 512], F32)
                v8 = sb3("v8", [128, 8 * HE], F32)
                vy8 = sb3("vy8", [128, 8 * HE], F32)
                combo = sb3("combo", [128, 64 * HE], F32)
                tau8 = sb3("tau8", [128, HE], F32)
                taub = sb3("taub", [128, HE], F32)
                e8 = sb3("e8", [128, 8 * HE], F32)
                zsum = sb3("zsum", [128, HE], F32)
                kt = sb3("kt", [128, HE], F32)
                exab = sb3("exab", [128, 64 * HE], BF16)
                exbb = sb3("exbb", [128, 64 * HE], BF16)
                d16 = sb3("d16", [128, 4096], I16)
                mkb = sb3("mkb", [128, 4096], BF16)
                prodb = sb3("prodb", [128, 4096], BF16)
                gacc = dbl("gacc", [128, 4096], BF16)
                tevac = dbl("tevac", [128, 512], BF16)
                ueb = dbl("ueb", [128, 1024], BF16)
                deb = dbl("deb", [128, 8 * 128], BF16)
                wsil = dbl("wsil", [128, CT], BF16)
                gld = dbl("gld", [128, CT], BF16)
                mld = dbl("mld", [128, CT], BF16)

                DV(lambda: V.memset(k_sub[0][64:67, :], 1.0), [k_sub[0]], [])
                DV(lambda: V.memset(k_sub[1][64:67, :], 1.0), [k_sub[1]], [])

                LD(kbd_sb[:].rearrange("p (c n) -> p c n", c=4),
                   kbd_d.rearrange("(c p) n -> p c n", p=128), "G")
                LD(bg_sb[:], bg_d)
                LD(bu_sb[:], bu_d)
                LD(bd_sb[:], bd_d)

                for mci in range(4):
                    mc = ch_list[mci]
                    nkc = (mc + 1) * 4
                    pending_tail = None
                    for h in range(H):
                        grp = h // 4
                        hl = h % 4
                        qb = q_sub[h % 2]
                        acc = [ps_c0, ps_c1][h % 2]
                        LD(qb[0:64, :],
                           q_dram[h * 64:(h + 1) * 64,
                                  mci * CT:(mci + 1) * CT])
                        LD(qb[64:67, :],
                           g_dram[:, h, mci * CT:(mci + 1) * CT])
                        nks = (nkc + 7) // 8
                        pend = None

                        def do_av(acc, hl, nkc):
                            pkb, ppt, pvsl, pkb0 = pend
                            p.op("T", lambda: T.matmul(
                                acc[0:65, 0:CT],
                                pvsl[:, (pkb - pkb0) * 260 + hl * 65:
                                     (pkb - pkb0) * 260 + hl * 65 + 65],
                                ppt[:], start=(pkb == 0),
                                stop=(pkb == nkc - 1)),
                                 outs=[acc],
                                 ins=[pvsl, ppt]
                                 + ([] if pkb == 0 else [acc]))

                        for ks in range(nks):
                            kb0 = ks * 8
                            kb1 = min(nkc, kb0 + 8)
                            kpar = (h * nks + ks) % 2
                            ksl = k_sub[kpar]
                            vsl = v_sub[kpar]
                            p.op("SY", lambda ksl=ksl, kb0=kb0, kb1=kb1,
                                 h=h: SY.dma_start(
                                     ksl[0:64, 0:(kb1 - kb0) * 128],
                                     k_dram[h * 64:(h + 1) * 64,
                                            kb0 * 128:kb1 * 128]),
                                 outs=[ksl], ins=[k_dram], inc=16)
                            p.op("SY", lambda vsl=vsl, kb0=kb0, kb1=kb1,
                                 grp=grp: SY.dma_start(
                                     vsl[:].rearrange("p (k c) -> p k c",
                                                      c=260)
                                     [:, 0:kb1 - kb0, :],
                                     v_dram[grp, kb0:kb1, :, :].rearrange(
                                         "k p c -> p k c")),
                                 outs=[vsl], ins=[v_dram], inc=16)
                            for kb in range(kb0, kb1):
                                sps = [ps_s0, ps_s1][kb % 2]
                                pt = p_t[kb % 3]
                                p.op("T", lambda sps=sps, ksl=ksl, kb=kb,
                                     kb0=kb0, qb=qb: T.matmul(
                                         sps[:, 0:CT],
                                         ksl[0:67, (kb - kb0) * 128:
                                             (kb - kb0 + 1) * 128],
                                         qb[0:67, :], start=True,
                                         stop=True),
                                     outs=[sps], ins=[ksl, qb])
                                p.op("SC", lambda pt=pt, sps=sps, kb=kb,
                                     h=h: SC.activation(
                                         pt[:], sps[:, 0:CT], AF.Exp,
                                         bias=kc[:, kb * 16 + h:
                                                 kb * 16 + h + 1],
                                         scale=INV8),
                                     outs=[pt], ins=[sps, kc])
                                dv = kb - mc * 4
                                if 0 <= dv < 4:
                                    DV(lambda pt=pt, dv=dv: V.tensor_tensor(
                                        pt[:], pt[:],
                                        cmask[:, dv * CT:(dv + 1) * CT],
                                        ALU.mult), [pt], [pt, cmask])
                                if kb == 0 and pending_tail is not None:
                                    pending_tail()
                                    pending_tail = None
                                if pend is not None:
                                    do_av(acc, hl, nkc)
                                pend = (kb, pt, vsl, kb0)
                        do_av(acc, hl, nkc)

                        def mk_tail(acc=acc, h=h):
                            def tail():
                                DV(lambda: V.reciprocal(
                                    Z_ROW, acc[64:65, 0:CT]),
                                   [rows], [acc])
                                MM(ps_b[0:64, 0:CT], ones1f[:, 0:64],
                                   Z_ROW, True, True)
                                DV(lambda: V.tensor_copy(
                                    rb64[:], ps_b[0:64, 0:CT]),
                                   [rb64], [ps_b])
                                oc, ob = h // 2, (h % 2) * 64
                                DV(lambda oc=oc, ob=ob: V.tensor_tensor(
                                    o_fm[ob:ob + 64,
                                         oc * CT:(oc + 1) * CT],
                                    acc[0:64, 0:CT], rb64[:], ALU.mult),
                                   [o_fm], [acc, rb64])
                            return tail
                        pending_tail = mk_tail()
                    pending_tail()
                    pending_tail = None

                    LD(xc[:].rearrange("p (c t) -> p c t", c=8),
                       xT.rearrange("(c p) t -> p c t", p=128)
                       [:, :, mc * CT:(mc + 1) * CT])
                    for dc in range(8):
                        wb = wob[dc % 2]
                        p.op("G", lambda wb=wb, dc=dc: G.dma_start(
                            wb[:].rearrange("p (c n) -> p c n", c=8),
                            wo_r[dc].rearrange("(c p) n -> p c n", p=128)),
                             outs=[wb], ins=[wo_r], inc=16)
                        for ic in range(8):
                            p.op("T", lambda wb=wb, ic=ic: T.matmul(
                                ps_a[:, 0:CT],
                                wb[:, ic * 128:(ic + 1) * 128],
                                o_fm[:, ic * CT:(ic + 1) * CT],
                                start=(ic == 0), stop=(ic == 7)),
                                 outs=[ps_a],
                                 ins=[wb, o_fm]
                                 + ([] if ic == 0 else [ps_a]))
                        DV(lambda dc=dc: V.tensor_tensor(
                            x2[:, dc * CT:(dc + 1) * CT], ps_a[:, 0:CT],
                            xc[:, dc * CT:(dc + 1) * CT], ALU.add),
                           [x2], [ps_a, xc])
                    if DEBUG:
                        LD(dbg_x2.rearrange("m (c p) t -> m p c t", p=128)
                           [mci], x2[:].rearrange("p (c t) -> p c t", c=8))
                    layernorm(x2[:], h_mc)

                    for dcd in range(32):
                        par = dcd % 2
                        wg_b = wgu[par]
                        gc, un = gch[par], uch[par]
                        psg = [ps_a, ps_s0][par]
                        psu = [ps_b, ps_s1][par]
                        p.op("G", lambda wg_b=wg_b, dcd=dcd: G.dma_start(
                            wg_b[:, 0:1024].rearrange("p (c n) -> p c n",
                                                      c=8),
                            wg_d.rearrange("(c p) n -> p c n", p=128)
                            [:, :, dcd * 128:(dcd + 1) * 128]),
                             outs=[wg_b], ins=[wg_d], inc=16)
                        p.op("G", lambda wg_b=wg_b, dcd=dcd: G.dma_start(
                            wg_b[:, 1024:2048].rearrange(
                                "p (c n) -> p c n", c=8),
                            wu_d.rearrange("(c p) n -> p c n", p=128)
                            [:, :, dcd * 128:(dcd + 1) * 128]),
                             outs=[wg_b], ins=[wu_d], inc=16)
                        for base, dst_ps in ((0, psg), (1024, psu)):
                            for dc in range(8):
                                p.op("T", lambda dst_ps=dst_ps, wg_b=wg_b,
                                     base=base, dc=dc: T.matmul(
                                         dst_ps[:, 0:CT],
                                         wg_b[:, base + dc * 128:
                                              base + dc * 128 + 128],
                                         h_mc[:, dc * CT:(dc + 1) * CT],
                                         start=(dc == 0), stop=(dc == 7)),
                                     outs=[dst_ps],
                                     ins=[wg_b, h_mc]
                                     + ([] if dc == 0 else [dst_ps]))
                        p.op("SC", lambda gc=gc, psg=psg, dcd=dcd:
                             SC.activation(gc[:], psg[:, 0:CT], AF.Silu,
                                           bias=bg_sb[:, dcd:dcd + 1]),
                             outs=[gc], ins=[psg, bg_sb])
                        p.op("SC", lambda un=un, psu=psu, dcd=dcd:
                             SC.activation(un[:], psu[:, 0:CT],
                                           AF.Identity,
                                           bias=bu_sb[:, dcd:dcd + 1]),
                             outs=[un], ins=[psu, bu_sb])
                        DV(lambda gc=gc, un=un: V.tensor_mul(
                            gc[:], gc[:], un[:]), [gc], [gc, un])
                        LD(m_dram[dcd * 128:(dcd + 1) * 128, :], gc[:])

                    for qc in range(4):
                        wq_b = wqb[qc % 2]
                        p.op("G", lambda wq_b=wq_b, qc=qc: G.dma_start(
                            wq_b[:].rearrange("p (c n) -> p c n", c=8),
                            wq_r[qc].rearrange("(c p) n -> p c n", p=128)),
                             outs=[wq_b], ins=[wq_r], inc=16)
                        for dc in range(8):
                            p.op("T", lambda wq_b=wq_b, dc=dc: T.matmul(
                                ps_a[:, 0:CT],
                                wq_b[:, dc * 128:(dc + 1) * 128],
                                h_mc[:, dc * CT:(dc + 1) * CT],
                                start=(dc == 0), stop=(dc == 7)),
                                 outs=[ps_a],
                                 ins=[wq_b, h_mc]
                                 + ([] if dc == 0 else [ps_a]))
                        ACT(qe_fm[:, qc * CT:(qc + 1) * CT], ps_a[:, 0:CT],
                            AF.Identity)
                    for tt in range(4):
                        for qc in range(4):
                            MM(ps_g[:, 0:512],
                               qe_fm[:, qc * CT + tt * 128:
                                     qc * CT + tt * 128 + 128],
                               kbd_sb[:, qc * 512:(qc + 1) * 512],
                               qc == 0, qc == 3)
                        sxq = sxq_t[tt]
                        ACT(sxf[:], ps_g[:, 0:512], AF.Identity, scale=QSC)
                        DV(lambda sxq=sxq: V.tensor_copy(sxq[:], sxf[:]),
                           [sxq], [sxf])

                    def transp_tt(tt):
                        ga = gacc[tt % 2]
                        for pp in range(8):
                            for ci in range(4):
                                ec = pp * 4 + ci
                                p.op("T", lambda ec=ec, ci=ci, ga=ga:
                                     T.matmul(
                                         ps_tp[:, ci * 128:(ci + 1) * 128],
                                         ga[:, ec * 128:(ec + 1) * 128],
                                         ident[:], start=True, stop=True),
                                     outs=[ps_tp],
                                     ins=[ga, ident]
                                     + ([] if ci == 0 else [ps_tp]))
                            tv = tevac[pp % 2]
                            ACT(tv[:], ps_tp[:], AF.Identity)
                            LD(gate_dram.rearrange("(c p) t -> p c t",
                                                   p=128)
                               [:, pp * 4:(pp + 1) * 4,
                                tt * 128:(tt + 1) * 128],
                               tv[:].rearrange("p (c t) -> p c t", c=4))

                    def a_block(ecs):
                        for ec in ecs:
                            de_b = deb[ec % 2]
                            ws_b = wsil[ec % 2]
                            psw = [ps_a, ps_b][ec % 2]
                            p.op("G", lambda de_b=de_b, ec=ec: G.dma_start(
                                de_b[:].rearrange("p (c n) -> p c n", c=8),
                                deT_d.rearrange("(c p) n -> p c n", p=128)
                                [:, :, ec * 128:(ec + 1) * 128]),
                                 outs=[de_b], ins=[deT_d], inc=16)
                            for dc in range(8):
                                p.op("T", lambda psw=psw, de_b=de_b, dc=dc:
                                     T.matmul(
                                         psw[:, 0:CT],
                                         de_b[:, dc * 128:(dc + 1) * 128],
                                         h_mc[:, dc * CT:(dc + 1) * CT],
                                         start=(dc == 0), stop=(dc == 7)),
                                     outs=[psw],
                                     ins=[de_b, h_mc]
                                     + ([] if dc == 0 else [psw]))
                            p.op("SC", lambda ws_b=ws_b, psw=psw:
                                 SC.activation(ws_b[:], psw[:, 0:CT],
                                               AF.Silu),
                                 outs=[ws_b], ins=[psw])
                            LD(a_dram[ec * 128:(ec + 1) * 128, :], ws_b[:])

                    for tt in range(4):
                        sxq = sxq_t[tt]
                        ga = gacc[tt % 2]
                        DV(lambda sxq=sxq: V.tensor_copy(sxf[:], sxq[:]),
                           [sxf], [sxq])
                        for he in range(HE):
                            DV(lambda he=he: V.max(
                                out=v8[:, he * 8:he * 8 + 8],
                                in_=sxf[:, he * 64:he * 64 + 64]),
                               [v8], [sxf])
                            DV(lambda he=he: V.max(
                                out=vy8[:, he * 8:he * 8 + 8],
                                in_=sxf[:, 256 + he * 64:
                                        256 + he * 64 + 64]),
                               [vy8], [sxf])
                            DV(lambda he=he: V.tensor_tensor(
                                combo[:, he * 64:(he + 1) * 64].rearrange(
                                    "p (a b) -> p a b", b=8),
                                v8[:, he * 8:he * 8 + 8].rearrange(
                                    "p (a o) -> p a o", o=1)
                                .to_broadcast([128, 8, 8]),
                                vy8[:, he * 8:he * 8 + 8].rearrange(
                                    "p (o b) -> p o b", o=1)
                                .to_broadcast([128, 8, 8]),
                                ALU.add), [combo], [v8, vy8])
                        for he in range(HE):
                            DV(lambda he=he: V.max(
                                out=v8[:, he * 8:he * 8 + 8],
                                in_=combo[:, he * 64:(he + 1) * 64]),
                               [v8], [combo])
                        for he in range(HE):
                            hs = slice(he * 8, he * 8 + 8)
                            DV(lambda he=he: V.tensor_copy(
                                tau8[:, he:he + 1],
                                v8[:, he * 8 + 7:he * 8 + 8]),
                               [tau8], [v8])
                            DV(lambda he=he, hs=hs: V.tensor_scalar(
                                e8[:, hs], v8[:, hs],
                                v8[:, he * 8:he * 8 + 1], None,
                                op0=ALU.subtract), [e8], [v8])
                            p.op("SC", lambda hs=hs: SC.activation(
                                e8[:, hs], e8[:, hs], AF.Exp,
                                scale=1.0 / QSC), outs=[e8], ins=[e8])
                            DV(lambda he=he, hs=hs: V.reduce_sum(
                                zsum[:, he:he + 1], e8[:, hs], axis=AXX),
                               [zsum], [e8])
                            DV(lambda he=he: V.tensor_scalar(
                                kt[:, he:he + 1], tau8[:, he:he + 1],
                                v8[:, he * 8:he * 8 + 1], None,
                                op0=ALU.subtract), [kt], [tau8, v8])
                            p.op("SC", lambda he=he: SC.activation(
                                kt[:, he:he + 1], kt[:, he:he + 1],
                                AF.Exp, scale=1.0 / QSC),
                                 outs=[kt], ins=[kt])
                            DV(lambda he=he: V.reciprocal(
                                zsum[:, he:he + 1], zsum[:, he:he + 1]),
                               [zsum], [zsum])
                            DV(lambda he=he: V.tensor_mul(
                                kt[:, he:he + 1], kt[:, he:he + 1],
                                zsum[:, he:he + 1]), [kt], [kt, zsum])
                            DV(lambda he=he: V.tensor_scalar_mul(
                                taub[:, he:he + 1], tau8[:, he:he + 1],
                                -1.0 / QSC), [taub], [tau8])
                            p.op("SC", lambda he=he: SC.activation(
                                exab[:, he * 64:(he + 1) * 64],
                                sxf[:, he * 64:he * 64 + 64], AF.Exp,
                                bias=taub[:, he:he + 1], scale=1.0 / QSC),
                                 outs=[exab], ins=[sxf, taub])
                            DV(lambda he=he: V.tensor_scalar(
                                exab[:, he * 64:(he + 1) * 64],
                                exab[:, he * 64:(he + 1) * 64],
                                kt[:, he:he + 1], None, op0=ALU.mult),
                               [exab], [exab, kt])
                            p.op("SC", lambda he=he: SC.activation(
                                exbb[:, he * 64:(he + 1) * 64],
                                sxf[:, 256 + he * 64:256 + he * 64 + 64],
                                AF.Exp, scale=1.0 / QSC),
                                 outs=[exbb], ins=[sxf])
                            DV(lambda he=he: V.tensor_tensor(
                                d16[:].rearrange("p (a b) -> p a b", b=64),
                                sxq[:, he * 64:he * 64 + 64].rearrange(
                                    "p (a o) -> p a o", o=1)
                                .to_broadcast([128, 64, 64]),
                                sxq[:, 256 + he * 64:256 + he * 64 + 64]
                                .rearrange("p (o b) -> p o b", o=1)
                                .to_broadcast([128, 64, 64]),
                                ALU.add), [d16], [sxq])
                            DV(lambda he=he: V.tensor_scalar(
                                mkb[:], d16[:], tau8[:, he:he + 1], None,
                                op0=ALU.is_ge), [mkb], [d16, tau8])
                            DV(lambda he=he: V.tensor_tensor(
                                prodb[:].rearrange("p (a b) -> p a b",
                                                   b=64),
                                exab[:, he * 64:(he + 1) * 64].rearrange(
                                    "p (a o) -> p a o", o=1)
                                .to_broadcast([128, 64, 64]),
                                exbb[:, he * 64:(he + 1) * 64].rearrange(
                                    "p (o b) -> p o b", o=1)
                                .to_broadcast([128, 64, 64]),
                                ALU.mult), [prodb], [exab, exbb])
                            dst = ga if he == 0 else prodb
                            DV(lambda dst=dst, he=he: V.tensor_mul(
                                dst[:], prodb[:], mkb[:]),
                               [dst], [prodb, mkb])
                            if he > 0:
                                DV(lambda ga=ga: V.tensor_add(
                                    ga[:], ga[:], prodb[:]),
                                   [ga], [ga, prodb])
                        if tt > 0:
                            transp_tt(tt - 1)
                        a_block(range(tt * 8, tt * 8 + 8))
                    transp_tt(3)

                    banks = [ps_a, ps_b, ps_s0, ps_s1, ps_c0, ps_c1,
                             ps_g, ps_tp]
                    for dcd in range(32):
                        par = dcd % 2
                        wd_b = wdb[par]
                        ml_b = mld[par]
                        p.op("G", lambda wd_b=wd_b, dcd=dcd:
                             G.dma_start(
                                 wd_b[:],
                                 wd_d[dcd * 128:(dcd + 1) * 128, :]),
                             outs=[wd_b], ins=[wd_d], inc=16)
                        LD(ml_b[:],
                           m_dram[dcd * 128:(dcd + 1) * 128, :])
                        for dc in range(8):
                            p.op("T", lambda dc=dc, wd_b=wd_b,
                                 ml_b=ml_b, dcd=dcd: T.matmul(
                                     banks[dc][:, 0:CT],
                                     wd_b[:, dc * 128:(dc + 1) * 128],
                                     ml_b[:], start=(dcd == 0),
                                     stop=False),
                                 outs=[banks[dc]],
                                 ins=[wd_b, ml_b]
                                 + ([] if dcd == 0 else [banks[dc]]))
                    for ec in range(32):
                        par = ec % 2
                        ue_b = ueb[par]
                        ml_b = mld[par]
                        gl_b = gld[par]
                        p.op("G", lambda ue_b=ue_b, ec=ec:
                             G.dma_start(
                                 ue_b[:],
                                 ue_d[ec * 128:(ec + 1) * 128, :]),
                             outs=[ue_b], ins=[ue_d], inc=16)
                        LD(ml_b[:],
                           a_dram[ec * 128:(ec + 1) * 128, :])
                        LD(gl_b[:],
                           gate_dram[ec * 128:(ec + 1) * 128, :])
                        DV(lambda ml_b=ml_b, gl_b=gl_b: V.tensor_mul(
                            ml_b[:], ml_b[:], gl_b[:]),
                           [ml_b], [ml_b, gl_b])
                        for dc in range(8):
                            p.op("T", lambda dc=dc, ue_b=ue_b,
                                 ml_b=ml_b, ec=ec: T.matmul(
                                     banks[dc][:, 0:CT],
                                     ue_b[:, dc * 128:(dc + 1) * 128],
                                     ml_b[:], start=False,
                                     stop=(ec == 31)),
                                 outs=[banks[dc]],
                                 ins=[ue_b, ml_b, banks[dc]])
                    for dc in range(8):
                        DV(lambda dc=dc: V.scalar_tensor_tensor(
                            xc[:, dc * CT:(dc + 1) * CT],
                            banks[dc][:, 0:CT],
                            bd_sb[:, dc:dc + 1],
                            x2[:, dc * CT:(dc + 1) * CT],
                            op0=ALU.add, op1=ALU.add),
                           [xc], [banks[dc], bd_sb, x2])
                    LD(out_d.rearrange("m (c p) t -> m p c t", p=128)[mci],
                       xc[:].rearrange("p (c t) -> p c t", c=8))

            p.finish("SY")

    return nc


_NC_CACHE = None
_LAST_RESULTS = None


def kernel(L0_attn_x, L0_attn_Wqkv, L0_attn_Wo, L0_attn_log_xi,
           L0_attn_pi_gate_logit, L0_attn_e_gate_logit, L0_ffn_Wg, L0_ffn_bg,
           L0_ffn_Wu, L0_ffn_bu, L0_ffn_Wd, L0_ffn_bd, L0_ffn_Wq, L0_ffn_keys,
           L0_ffn_down_embed, L0_ffn_up_embed):
    global _NC_CACHE, _LAST_RESULTS

    x = np.asarray(L0_attn_x, np.float32)
    Wqkv = np.asarray(L0_attn_Wqkv, np.float32)

    perm = np.empty(3 * D, np.int64)
    for g in range(4):
        for half in range(2):
            for hh in range(4):
                for f in range(32):
                    c = g * 256 + half * 128 + hh * 32 + f
                    row = (4 * g + hh) * 64 + 2 * f + half
                    perm[c] = row
                    perm[D + c] = D + row
    perm[2 * D:] = np.arange(2 * D, 3 * D)
    wqkvT = np.ascontiguousarray(Wqkv[perm].T)

    woT = np.asarray(L0_attn_Wo, np.float32).T
    wo_r = np.ascontiguousarray(woT.reshape(D, 8, 128).transpose(1, 0, 2))
    gates3 = np.ascontiguousarray(np.stack(
        [np.asarray(L0_attn_log_xi, np.float32),
         np.asarray(L0_attn_pi_gate_logit, np.float32),
         np.asarray(L0_attn_e_gate_logit, np.float32)], axis=1))

    wg = np.ascontiguousarray(np.asarray(L0_ffn_Wg, np.float32))
    wu = np.ascontiguousarray(np.asarray(L0_ffn_Wu, np.float32))
    wd = np.ascontiguousarray(np.asarray(L0_ffn_Wd, np.float32))
    bg2 = np.ascontiguousarray(
        np.asarray(L0_ffn_bg, np.float32).reshape(32, 128).T)
    bu2 = np.ascontiguousarray(
        np.asarray(L0_ffn_bu, np.float32).reshape(32, 128).T)
    bd2 = np.ascontiguousarray(
        np.asarray(L0_ffn_bd, np.float32).reshape(8, 128).T)
    wq = np.asarray(L0_ffn_Wq, np.float32)
    wq_r = np.ascontiguousarray(wq.reshape(D, 4, 128).transpose(1, 0, 2))
    keys = np.asarray(L0_ffn_keys, np.float32)
    kbd = np.zeros((512, 512), np.float32)
    for pp in range(2):
        for he in range(HE):
            r0 = pp * 256 + he * 64
            kbd[r0:r0 + 64, r0:r0 + 64] = keys[he, :, pp, :].T
    deT = np.ascontiguousarray(np.asarray(L0_ffn_down_embed, np.float32).T)
    ue = np.ascontiguousarray(np.asarray(L0_ffn_up_embed, np.float32))

    import jax
    import jax.numpy as jnp
    with jax.default_device(jax.local_devices(backend="cpu")[0]):
        kfrac_j = jnp.arange(0, 64, 2, dtype=jnp.float32) / 64
        pi_inv_j = jnp.float32(math.pi) ** (1.0 - kfrac_j)
        pos_j = jnp.arange(S, dtype=jnp.float32)
        th_np = np.asarray(pos_j[:, None] * pi_inv_j[None, :])
    th0 = np.ascontiguousarray(np.tile(th_np.T, (4, 1)).astype(np.float32))

    kpos = np.ascontiguousarray(
        (np.arange(128, dtype=np.float32)[:, None]
         + 128.0 * np.arange(64, dtype=np.float32)[None, :]))

    rselm = np.zeros((16, 4 * 128), np.float32)
    for g in range(4):
        for r in range(128):
            rselm[4 * g + r // 32, g * 128 + r] = 1.0

    cmask = np.zeros((128, 4 * CT), ml_dtypes.bfloat16)
    for dv in range(4):
        for pp in range(128):
            cmask[pp, dv * CT + dv * 128 + pp:(dv + 1) * CT] = 1.0

    if _NC_CACHE is None:
        _NC_CACHE = build_nc()
    nc = _NC_CACHE

    shared = dict(
        wqkvT=wqkvT, wo_r=wo_r, gates3=gates3, wg_d=wg, wu_d=wu, wd_d=wd,
        bg_d=bg2, bu_d=bu2, bd_d=bd2, wq_r=wq_r, kbd_d=kbd, deT_d=deT,
        ue_d=ue, th0_d=th0, kpos_d=kpos, rsel_d=rselm, cmask_d=cmask,
        ident_d=np.eye(128, dtype=ml_dtypes.bfloat16),
    )
    in_maps = []
    for core in range(8):
        b, bc = core // 4, core % 4
        qp = np.concatenate(
            [np.arange(mcc * CT, (mcc + 1) * CT, dtype=np.float32)
             for mcc in CHUNKS[bc]])
        in_maps.append(dict(
            shared,
            xT=np.ascontiguousarray(x[b].T),
            qpos_d=np.ascontiguousarray(np.tile(qp[None, :], (16, 1))),
        ))

    res = run_bass_kernel_spmd(nc, in_maps, core_ids=list(range(8)))
    _LAST_RESULTS = res

    out = np.empty((B, S, D), np.float32)
    for core in range(8):
        b, bc = core // 4, core % 4
        o = res.results[core]["out"]
        for mci, mcc in enumerate(CHUNKS[bc]):
            out[b, mcc * CT:(mcc + 1) * CT, :] = o[mci].T
    return out
